# version 03
# baseline (speedup 1.0000x reference)
import numpy as np
import concourse.bass as bass
import concourse.mybir as mybir

F32 = mybir.dt.float32
BF16 = mybir.dt.bfloat16
I32 = mybir.dt.int32
AF = mybir.ActivationFunctionType
ALU = mybir.AluOpType

ENGS = ("sync", "gpsimd", "vector", "scalar", "tensor")


class KB:
    def __init__(self, name="kern"):
        self.nc = bass.Bass("TRN2", target_bir_lowering=False)
        self.ops = []
        self._sems = {}
        self.count = {}
        self._ctxs = []
        self._scopes = []
        self._nameseq = 0

    def uniq(self, prefix):
        self._nameseq += 1
        return f"{prefix}{self._nameseq}"

    def input(self, name, shape, dtype):
        return self.nc.dram_tensor(name, shape, dtype, kind="ExternalInput").ap()

    def output(self, name, shape, dtype):
        return self.nc.dram_tensor(name, shape, dtype, kind="ExternalOutput").ap()

    def dram(self, name, shape, dtype):
        return self.nc.dram_tensor(name, shape, dtype, kind="Internal").ap()

    def sbuf(self, name, shape, dtype):
        g = self.nc.sbuf_tensor(name, shape, dtype)
        t = g.__enter__()
        (self._scopes[-1] if self._scopes else self._ctxs).append(g)
        return t

    def psum(self, name, shape, dtype=F32):
        g = self.nc.psum_tensor(name, shape, dtype)
        t = g.__enter__()
        (self._scopes[-1] if self._scopes else self._ctxs).append(g)
        return t

    def push_scope(self):
        self._scopes.append([])

    def pop_scope(self):
        for g in reversed(self._scopes.pop()):
            g.__exit__(None, None, None)

    def barrier(self):
        waits = [(s, self.count[s]) for s in list(self.count) if s != "bar"]
        self.op("gpsimd", lambda g: g.engine_nop(), wait=waits, inc=[("bar", 1)])
        b = ("bar", self.count["bar"])
        for eng in ("sync", "vector", "scalar", "tensor"):
            self.op(eng, None, wait=[b])
        return b

    def sem(self, name):
        if name not in self._sems:
            g = self.nc.semaphore(name)
            h = g.__enter__()
            self._ctxs.append(g)
            self._sems[name] = h
            self.count[name] = 0
        return self._sems[name]

    def op(self, engine, fn, wait=(), inc=()):
        self.sem_touch(wait, inc)
        waits = [(w[0], w[1]) for w in wait]
        incs = []
        for sname, delta in inc:
            self.count[sname] += delta
            incs.append((sname, delta))
        self.ops.append((engine, fn, waits, incs))
        return {s: self.count[s] for s, _ in incs}

    def sem_touch(self, wait, inc):
        for s, _ in list(wait) + list(inc):
            self.sem(s)

    def dma(self, engine, out, in_, wait=(), sname=None, **kw):
        sname = sname or ("dq_" + engine)
        self.sem(sname)
        prev = self.count[sname]
        w = list(wait)
        if prev > 0:
            w.append((sname, prev))
        self.op(engine, lambda e: e.dma_start(out=out, in_=in_, **kw),
                wait=w, inc=[(sname, 16)])
        return (sname, self.count[sname])

    def dma_raw(self, engine, fn, wait=(), sname=None):
        sname = sname or ("dq_" + engine)
        self.sem(sname)
        prev = self.count[sname]
        w = list(wait)
        if prev > 0:
            w.append((sname, prev))
        self.op(engine, fn, wait=w, inc=[(sname, 16)])
        return (sname, self.count[sname])

    def now(self, sname):
        self.sem(sname)
        return self.count[sname]

    def finalize(self, final_waits=()):
        nc = self.nc
        blk_g = nc.Block()
        blk = blk_g.__enter__()
        self._ctxs.append(blk_g)
        ops = self.ops
        sems = self._sems

        def emit(engobj, engname):
            for engine, fn, waits, incs in ops:
                if engine != engname:
                    continue
                for sname, val in waits:
                    engobj.wait_ge(sems[sname], val)
                if fn is None:
                    continue
                inst = fn(engobj)
                for sname, delta in incs:
                    inst.then_inc(sems[sname], delta)

        @blk.sync
        def _(e):
            emit(e, "sync")

        @blk.vector
        def _(e):
            emit(e, "vector")

        @blk.scalar
        def _(e):
            emit(e, "scalar")

        @blk.tensor
        def _(e):
            emit(e, "tensor")

        @blk.gpsimd
        def _(e):
            emit(e, "gpsimd")
            for sname, val in final_waits:
                e.wait_ge(sems[sname], val)

        return nc

    def close(self):
        for g in reversed(self._ctxs):
            try:
                g.__exit__(None, None, None)
            except Exception:
                pass


"""Phase emitters for the megakernel. Each emitter appends ops to a KB.

Layout conventions (per core):
  - feature-major (fm): DRAM [1024, NTOK] / SBUF tiles [128, ...]; d = 128*c + p.
  - token-major (tm): DRAM [NTOK, 1024]; SBUF tiles [128 tok, 1024].
  - NTOK = 4096 (2 sequences x 2048), token j = b_local*2048 + t.
"""
AX = mybir.AxisListType.X

NTOK = 4096
D = 1024
DC = D // 128


class Common:
    def __init__(self, kb: KB):
        self.kb = kb
        nc = kb.nc
        self.ones_f32_col = kb.sbuf("c_ones_f32c", [128, 1], F32)
        self.ones_f32_row = kb.sbuf("c_ones_f32r", [1, 128], F32)
        self.ones_bf_row = kb.sbuf("c_ones_bfr", [1, 128], BF16)
        self.ident_bf = kb.sbuf("c_ident_bf", [128, 128], BF16)
        self.ident_f32 = kb.sbuf("c_ident_f32", [128, 128], F32)
        self.tril = kb.sbuf("c_tril", [128, 128], F32)
        self.eps1 = kb.sbuf("c_eps1", [1, 1], F32)
        self.scratch_f = kb.sbuf("c_scratch_f", [128, 128], F32)
        self.scratch_f2 = kb.sbuf("c_scratch_f2", [128, 1], F32)

        kb.op("vector", lambda v: v.memset(self.ones_f32_col[:], 1.0), inc=[("chv", 1)])
        kb.op("vector", lambda v: v.memset(self.ones_f32_row[:], 1.0), inc=[("chv", 1)])
        kb.op("vector", lambda v: v.memset(self.ones_bf_row[:], 1.0), inc=[("chv", 1)])
        kb.op("vector", lambda v: v.memset(self.eps1[:], 1e-5), inc=[("chv", 1)])
        kb.op("gpsimd", lambda g: g.iota(self.scratch_f[:], pattern=[[1, 128]], base=0,
                                         channel_multiplier=0, allow_small_or_imprecise_dtypes=True),
              inc=[("chp", 1)])
        kb.op("gpsimd", lambda g: g.iota(self.scratch_f2[:], pattern=[[0, 1]], base=0,
                                         channel_multiplier=1, allow_small_or_imprecise_dtypes=True),
              inc=[("chp", 1)])
        kb.op("vector", lambda v: v.tensor_scalar(self.ident_bf[:], self.scratch_f[:],
                                                  self.scratch_f2[:], None, ALU.is_equal),
              wait=[("chp", kb.now("chp")), ("chv", kb.now("chv"))], inc=[("chv", 1)])
        kb.op("vector", lambda v: v.tensor_scalar(self.ident_f32[:], self.scratch_f[:],
                                                  self.scratch_f2[:], None, ALU.is_equal),
              wait=[("chv", kb.now("chv"))], inc=[("chv", 1)])
        kb.op("vector", lambda v: v.tensor_scalar(self.tril[:], self.scratch_f[:],
                                                  self.scratch_f2[:], None, ALU.is_gt),
              wait=[("chv", kb.now("chv"))], inc=[("chv", 1)])
        self.ready = ("chv", kb.now("chv"))


def emit_ln(kb: KB, cm: Common, x_fm, xn_fm, xn_tm, xn_tm_lo=None, tag="ln"):
    nc = kb.nc
    T = 512
    ntiles = NTOK // T
    u = kb.uniq(tag)
    xt = kb.sbuf(u + "xt", [128, DC * T], F32)
    sq = kb.sbuf(u + "sq", [128, T], F32)
    xnf = kb.sbuf(u + "xnf", [128, DC * T], BF16)
    xnt = kb.sbuf(u + "xnt", [128, DC * 128], BF16)
    if xn_tm_lo is not None:
        xnl = kb.sbuf(u + "xnl", [128, DC * T], BF16)
        xnlt = kb.sbuf(u + "xnlt", [128, DC * 128], BF16)
        x32 = kb.sbuf(u + "x32", [128, T], F32)
    mrow = kb.sbuf(u + "mrow", [1, T], F32)
    vrow = kb.sbuf(u + "vrow", [1, T], F32)
    rrow = kb.sbuf(u + "rrow", [1, T], F32)
    lrow = kb.sbuf(u + "lrow", [1, T], F32)
    tf = kb.sbuf(u + "tf", [128, T], F32)
    psA = kb.psum(u + "psA", [1, T], F32)
    psB = kb.psum(u + "psB", [1, T], F32)
    psM = kb.psum(u + "psM", [128, T], F32)
    psR = kb.psum(u + "psR", [128, T], F32)
    psT = kb.psum(u + "psT", [128, 128], BF16)

    x_v = x_fm.rearrange("(c p) t -> p c t", p=128)
    if xn_fm is not None:
        xnf_v = xn_fm.rearrange("(c p) t -> p c t", p=128)
    if xn_tm is not None:
        xnt_v = xn_tm

    for i in range(ntiles):
        ld = kb.dma("sync", xt[:].rearrange("p (c t) -> p c t", t=T), x_v[:, :, i * T:(i + 1) * T],
                    wait=[("chv", kb.now("chv")), ("pe", kb.now("pe")), ("chs", kb.now("chs")),
                          ("peB", kb.now("peB")), ("dq_gpsimd", kb.now("dq_gpsimd"))])
        for c in range(DC):
            kb.op("tensor", lambda t, c=c, i=i: t.matmul(
                psA[:], cm.ones_f32_col[:], xt[:, c * T:(c + 1) * T],
                start=(c == 0), stop=(c == DC - 1)),
                wait=[ld] if c == 0 else (), inc=[("pe", 1)] if c == DC - 1 else ())
        for c in range(DC):
            sqd = kb.op("scalar", lambda s, c=c: s.activation(sq[:], xt[:, c * T:(c + 1) * T], AF.Square),
                        wait=[ld, ("pe", kb.now("pe") + (1 if c > 0 else 0))] if False else [ld, ("chs", kb.now("chs")), ("peB", kb.now("peB"))],
                        inc=[("chs", 1)])
            kb.op("tensor", lambda t, c=c: t.matmul(
                psB[:], cm.ones_f32_col[:], sq[:],
                start=(c == 0), stop=(c == DC - 1)),
                wait=[("chs", kb.now("chs"))], inc=[("peB", 1)])
        peB_done = ("peB", kb.now("peB"))
        pe_done = ("pe", kb.now("pe"))
        kb.op("vector", lambda v: v.tensor_scalar(mrow[:], psA[:], 1.0 / D, None, ALU.mult),
              wait=[pe_done, peB_done, ("chv", kb.now("chv"))], inc=[("chv", 1)])
        kb.op("vector", lambda v: v.tensor_scalar(vrow[:], psB[:], 1.0 / D, None, ALU.mult),
              wait=[("chv", kb.now("chv"))], inc=[("chv", 1)])
        kb.op("vector", lambda v: v.tensor_tensor(tf[0:1, :], mrow[:], mrow[:], ALU.mult),
              wait=[("chv", kb.now("chv"))], inc=[("chv", 1)])
        kb.op("vector", lambda v: v.tensor_tensor(vrow[:], vrow[:], tf[0:1, :], ALU.subtract),
              wait=[("chv", kb.now("chv"))], inc=[("chv", 1)])
        vd = ("chv", kb.now("chv"))
        kb.op("scalar", lambda s: s.activation(lrow[:], vrow[:], AF.Ln, bias=cm.eps1[:]),
              wait=[vd, ("chs", kb.now("chs"))], inc=[("chs", 1)])
        kb.op("scalar", lambda s: s.activation(rrow[:], lrow[:], AF.Exp, scale=-0.5),
              wait=[("chs", kb.now("chs"))], inc=[("chs", 1)])
        rd = ("chs", kb.now("chs"))
        kb.op("tensor", lambda t: t.matmul(psM[:], cm.ones_f32_row[:], mrow[:], start=True, stop=True),
              wait=[vd], inc=[("pe", 1)])
        kb.op("tensor", lambda t: t.matmul(psR[:], cm.ones_f32_row[:], rrow[:], start=True, stop=True),
              wait=[rd], inc=[("pe", 1)])
        bc_done = ("pe", kb.now("pe"))
        for c in range(DC):
            kb.op("vector", lambda v, c=c: v.tensor_tensor(tf[:], xt[:, c * T:(c + 1) * T], psM[:], ALU.subtract),
                  wait=[bc_done, ("chv", kb.now("chv"))], inc=[("chv", 1)])
            if xn_tm_lo is None:
                kb.op("vector", lambda v, c=c: v.tensor_tensor(xnf[:, c * T:(c + 1) * T], tf[:], psR[:], ALU.mult),
                      wait=[("chv", kb.now("chv"))], inc=[("chv", 1)])
            else:
                kb.op("vector", lambda v, c=c: v.tensor_tensor(x32[:], tf[:], psR[:], ALU.mult),
                      wait=[("chv", kb.now("chv"))], inc=[("chv", 1)])
                kb.op("vector", lambda v, c=c: v.tensor_copy(xnf[:, c * T:(c + 1) * T], x32[:]),
                      wait=[("chv", kb.now("chv"))], inc=[("chv", 1)])
                kb.op("vector", lambda v, c=c: v.tensor_tensor(xnl[:, c * T:(c + 1) * T], x32[:],
                                                               xnf[:, c * T:(c + 1) * T], ALU.subtract),
                      wait=[("chv", kb.now("chv"))], inc=[("chv", 1)])
        nrm_done = ("chv", kb.now("chv"))
        if xn_fm is not None:
            kb.dma("gpsimd", xnf_v[:, :, i * T:(i + 1) * T],
                   xnf[:].rearrange("p (c t) -> p c t", t=T), wait=[nrm_done])
        if xn_tm is not None:
            planes = [(xnf, xnt, xn_tm)]
            if xn_tm_lo is not None:
                planes.append((xnl, xnlt, xn_tm_lo))
            for src_t, dst_t, dram_t in planes:
                for b in range(T // 128):
                    war = ("dq_gpsimd", kb.now("dq_gpsimd"))
                    for c in range(DC):
                        kb.op("tensor", lambda t, b=b, c=c, src_t=src_t: t.transpose(
                            psT[:], src_t[:, c * T + b * 128: c * T + (b + 1) * 128], cm.ident_bf[:]),
                            wait=[nrm_done, ("chv", kb.now("chv"))], inc=[("pe", 1)])
                        kb.op("vector", lambda v, c=c, dst_t=dst_t: v.tensor_copy(dst_t[:, c * 128:(c + 1) * 128], psT[:]),
                              wait=[("pe", kb.now("pe")), ("chv", kb.now("chv")), war], inc=[("chv", 1)])
                    kb.dma("gpsimd", dram_t[i * T + b * 128:i * T + (b + 1) * 128, :], dst_t[:],
                           wait=[("chv", kb.now("chv"))])
        if i < ntiles - 1:
            kb.op("vector", lambda v: v.engine_nop(),
                  wait=[("dq_gpsimd", kb.now("dq_gpsimd"))], inc=[("chv", 1)])
    return


def emit_conv(kb: KB, cm: Common, xn_fm, xn_tm, freqs_c, decays_c, in_projT, out_proj,
              ss_out_fm, tag="cv"):
    nc = kb.nc
    u = kb.uniq(tag)
    Gg, C = 32, 512
    NSEQ, L = 2, 2048
    NPOW = 513
    JS = C // Gg
    NC512 = L // C

    def V(fn, w=()):
        kb.op("vector", fn, wait=[("chv", kb.now("chv"))] + list(w), inc=[("chv", 1)])
        return ("chv", kb.now("chv"))

    def SC(fn, w=()):
        kb.op("scalar", fn, wait=[("chs", kb.now("chs"))] + list(w), inc=[("chs", 1)])
        return ("chs", kb.now("chs"))

    def T(fn, w=(), inc=True):
        kb.op("tensor", fn, wait=list(w), inc=[("pe", 1)] if inc else ())
        return ("pe", kb.now("pe"))

    fq = kb.sbuf(u + "fq", [64, 1], F32)
    dc_ = kb.sbuf(u + "dc", [64, 1], F32)
    ipT = kb.sbuf(u + "ipT", [64, D], F32)
    opj = kb.sbuf(u + "opj", [64, D], F32)
    w2z = kb.sbuf(u + "w2z", [128, D], BF16)
    w2x2 = kb.sbuf(u + "w2x2", [128, D], F32)
    t64a = kb.sbuf(u + "t64a", [64, 1], F32)
    t64b = kb.sbuf(u + "t64b", [64, 1], F32)
    t64c = kb.sbuf(u + "t64c", [64, 1], F32)
    amp = kb.sbuf(u + "amp", [64, 1], F32)
    ar = kb.sbuf(u + "ar", [64, 1], F32)
    ai = kb.sbuf(u + "ai", [64, 1], F32)
    curr = kb.sbuf(u + "curr", [64, 2], F32)
    P = kb.sbuf(u + "P", [64, 2 * NPOW], F32)
    MB = kb.sbuf(u + "MB", [128, NPOW], BF16)
    PB = kb.sbuf(u + "PB", [128, NPOW], BF16)
    A32 = kb.sbuf(u + "A32", [32, 128], BF16)
    zrow = kb.sbuf(u + "zrow", [1, D], BF16)
    A512 = kb.sbuf(u + "A512", [128, 4 * 128], BF16)
    ACT_ = kb.sbuf(u + "ACT", [128, 128], F32)
    rowmodf = kb.sbuf(u + "rowmodf", [128, 1], F32)
    mask64 = kb.sbuf(u + "mask64", [128, 64], I32)
    a5r = kb.sbuf(u + "a5r", [128, 1], F32)
    a5i = kb.sbuf(u + "a5i", [128, 1], F32)
    a5in = kb.sbuf(u + "a5in", [128, 1], F32)
    K32T = kb.sbuf(u + "K32T", [128, DC * 32], BF16)
    zzb = kb.sbuf(u + "zzb", [128, 128], BF16)
    zf = kb.sbuf(u + "zf", [128, 64], F32)
    zbf = kb.sbuf(u + "zbf", [128, 128], BF16)
    diags = kb.sbuf(u + "diags", [128, 32 * 128], BF16)
    psS = kb.psum(u + "psS", [128, D], F32)
    psCt = kb.psum(u + "psCt", [128, D], F32)
    psN = kb.psum(u + "psN", [128, 512], F32)
    psTb = kb.psum(u + "psTb", [128, 512], BF16)
    psK = psS[0:32, :]
    psT2 = psTb[:, 0:128]

    kb.dma("sync", fq[:], freqs_c)
    kb.dma("sync", dc_[:], decays_c)
    kb.dma("sync", ipT[:], in_projT)
    l4 = kb.dma("sync", opj[:], out_proj)
    lds = l4

    wv = V(lambda v: v.tensor_tensor(w2x2[0:64, :], ipT[:], opj[:], ALU.mult), w=[lds])
    dw = kb.dma("gpsimd", w2x2[64:128, :], w2x2[0:64, :], wait=[wv])
    V(lambda v: v.memset(w2z[64:128, :], 0.0))
    V(lambda v: v.tensor_copy(w2z[0:64, :], w2x2[0:64, :]), w=[wv])

    SC(lambda s: s.activation(t64a[:], dc_[:], AF.Exp), w=[lds])
    s2 = SC(lambda s: s.activation(amp[:], t64a[:], AF.Exp, scale=-1.0))
    TWO_PI = float(2.0 * np.pi)
    V(lambda v: v.tensor_scalar(t64b[:], fq[:], TWO_PI, None, ALU.mult), w=[lds])
    V(lambda v: v.tensor_scalar(t64c[:], t64b[:], 1.0 / TWO_PI, None, ALU.mult))
    M23 = float(1.5 * 2.0 ** 23)
    V(lambda v: v.tensor_scalar(t64c[:], t64c[:], M23, M23, ALU.add, ALU.subtract))
    v4 = V(lambda v: v.scalar_tensor_tensor(t64c[:], t64c[:], -TWO_PI, t64b[:], ALU.mult, ALU.add))
    s3 = SC(lambda s: s.activation(t64a[:], t64c[:], AF.Sin), w=[v4])
    V(lambda v: v.tensor_scalar(t64b[:], t64c[:], float(np.pi / 2), None, ALU.add))
    V(lambda v: v.tensor_scalar(t64c[:], t64b[:], float(np.pi), None, ALU.is_gt), w=[s3])
    v7 = V(lambda v: v.scalar_tensor_tensor(t64b[:], t64c[:], -TWO_PI, t64b[:], ALU.mult, ALU.add))
    s4 = SC(lambda s: s.activation(t64c[:], t64b[:], AF.Sin), w=[v7])
    V(lambda v: v.tensor_tensor(ar[:], amp[:], t64c[:], ALU.mult), w=[s4, s2])
    V(lambda v: v.tensor_tensor(ai[:], amp[:], t64a[:], ALU.mult), w=[s3])

    V(lambda v: v.memset(P[:, 0:1], 1.0))
    V(lambda v: v.memset(P[:, NPOW:NPOW + 1], 0.0))
    V(lambda v: v.tensor_copy(P[:, 1:2], ar[:]))
    V(lambda v: v.tensor_copy(P[:, NPOW + 1:NPOW + 2], ai[:]))
    V(lambda v: v.tensor_copy(curr[:, 0:1], ar[:]))
    V(lambda v: v.tensor_copy(curr[:, 1:2], ai[:]))
    m = 1
    RE, IM = 0, NPOW
    while m < NPOW:
        w_ = min(m, NPOW - m)
        V(lambda v, m=m, w_=w_: v.tensor_scalar(P[:, RE + m:RE + m + w_], P[:, IM + 0:IM + 0 + w_], curr[:, 1:2], None, ALU.mult))
        V(lambda v, m=m, w_=w_: v.scalar_tensor_tensor(P[:, RE + m:RE + m + w_], P[:, RE + 0:RE + 0 + w_], curr[:, 0:1], P[:, RE + m:RE + m + w_], ALU.mult, ALU.subtract))
        V(lambda v, m=m, w_=w_: v.tensor_scalar(P[:, IM + m:IM + m + w_], P[:, RE + 0:RE + 0 + w_], curr[:, 1:2], None, ALU.mult))
        V(lambda v, m=m, w_=w_: v.scalar_tensor_tensor(P[:, IM + m:IM + m + w_], P[:, IM + 0:IM + 0 + w_], curr[:, 0:1], P[:, IM + m:IM + m + w_], ALU.mult, ALU.add))
        V(lambda v: v.tensor_tensor(t64a[:], curr[:, 0:1], curr[:, 0:1], ALU.mult))
        V(lambda v: v.tensor_tensor(t64b[:], curr[:, 1:2], curr[:, 1:2], ALU.mult))
        V(lambda v: v.tensor_tensor(t64c[:], curr[:, 0:1], curr[:, 1:2], ALU.mult))
        V(lambda v: v.tensor_tensor(curr[:, 0:1], t64a[:], t64b[:], ALU.subtract))
        V(lambda v: v.tensor_scalar(curr[:, 1:2], t64c[:], 2.0, None, ALU.mult))
        m *= 2
    ptab = ("chv", kb.now("chv"))

    V(lambda v: v.tensor_copy(MB[0:64, :], P[:, 0:NPOW]), w=[ptab])
    V(lambda v: v.tensor_copy(PB[0:64, :], P[:, 0:NPOW]))
    d2 = kb.dma("gpsimd", MB[64:128, :], P[:, NPOW:2 * NPOW], wait=[ptab])
    d3 = kb.dma("gpsimd", PB[64:128, :], P[:, NPOW:2 * NPOW], wait=[ptab])
    V(lambda v: v.tensor_scalar(MB[64:128, :], MB[64:128, :], -1.0, None, ALU.mult), w=[d2])
    MBZ = kb.sbuf(u + "MBZ", [128, 64], BF16)
    V(lambda v: v.memset(MBZ[:, 0:32], 0.0))
    V(lambda v: v.tensor_copy(MBZ[:, 32:64], MB[:, 1:33]))
    mbready = ("chv", kb.now("chv"))
    pbready = d3

    rv = V(lambda v: v.tensor_copy(zzb[:, 0:32], PB[:, 31::-1]), w=[pbready])
    tr1 = T(lambda t: t.transpose(psT2[0:32, :], zzb[:, 0:32], cm.ident_bf[:]), w=[rv])
    a32r = V(lambda v: v.tensor_copy(A32[:, :], psT2[0:32, :]), w=[tr1])
    V(lambda v: v.memset(zrow[:], 0.0))
    for k in range(4):
        hi = 511 - 128 * k
        rv2 = V(lambda v, hi=hi: v.tensor_copy(zzb[:, 0:128], PB[:, hi:hi - 128:-1] if hi - 128 >= 0 else PB[:, hi::-1]), w=[pbready, ("pe", kb.now("pe"))])
        tr2 = T(lambda t: t.transpose(psT2[:], zzb[:, 0:128], cm.ident_bf[:]), w=[rv2])
        V(lambda v, k=k: v.tensor_copy(A512[:, 128 * k:128 * (k + 1)], psT2[:]), w=[tr2])
    abready = ("chv", kb.now("chv"))

    kb.op("gpsimd", lambda g: g.iota(rowmodf[:], pattern=[[0, 1]], base=0, channel_multiplier=1,
                                     allow_small_or_imprecise_dtypes=True), inc=[("chp", 1)])
    rm = ("chp", kb.now("chp"))
    kb.op("gpsimd", lambda g: g.iota(zf[:, 0:64], pattern=[[1, 64]], base=0, channel_multiplier=0,
                                     allow_small_or_imprecise_dtypes=True), inc=[("chp", 1)])
    ci = ("chp", kb.now("chp"))
    V(lambda v: v.tensor_scalar(a5r[:].bitcast(I32), rowmodf[:].bitcast(I32), 0, None, ALU.bitwise_and), w=[rm])
    V(lambda v: v.tensor_scalar(a5i[:], rowmodf[:], 64.0, None, ALU.is_ge), w=[rm])
    V(lambda v: v.scalar_tensor_tensor(rowmodf[:], a5i[:], -64.0, rowmodf[:], ALU.mult, ALU.add))
    rmq = V(lambda v: v.tensor_scalar(mask64[:], zf[:, 0:64], rowmodf[:], None, ALU.is_equal), w=[ci])
    V(lambda v: v.tensor_copy(a5r[0:64, :], P[:, 512:513]), w=[ptab])
    V(lambda v: v.tensor_copy(a5i[0:64, :], P[:, NPOW + 512:NPOW + 513]))
    vv = ("chv", kb.now("chv"))
    kb.dma("gpsimd", a5r[64:128, :], P[:, 512:513], wait=[ptab])
    d5 = kb.dma("gpsimd", a5i[64:128, :], P[:, NPOW + 512:NPOW + 513], wait=[ptab])
    nai = V(lambda v: v.tensor_scalar(a5in[:], a5i[:], -1.0, None, ALU.mult), w=[d5])
    V(lambda v: v.memset(zf[:, 0:64], 0.0), w=[rmq])
    V(lambda v: v.select(ACT_[:, 0:64], mask64[:], a5r[:].to_broadcast([128, 64]), zf[:, 0:64]), w=[d5])
    V(lambda v: v.select(ACT_[64:128, 0:64], mask64[64:128, :], a5in[64:128, :].to_broadcast([64, 64]), zf[64:128, 0:64]), w=[nai])
    V(lambda v: v.select(ACT_[:, 64:128], mask64[:], a5i[:].to_broadcast([128, 64]), zf[:, 0:64]))
    V(lambda v: v.select(ACT_[64:128, 64:128], mask64[64:128, :], a5r[64:128, :].to_broadcast([64, 64]), zf[64:128, 0:64]))
    actready = ("chv", kb.now("chv"))

    T(lambda t: t.matmul(psK[:, 0:512], MB[:, 0:32], w2z[:, 0:512], start=True, stop=True),
      w=[mbready, ("chv", kb.now("chv"))])
    tk = T(lambda t: t.matmul(psK[:, 512:1024], MB[:, 0:32], w2z[:, 512:1024], start=True, stop=True))
    kev = kb.sbuf(u + "kev", [32, D], BF16)
    kc2 = V(lambda v: v.tensor_copy(kev[:], psK[:]), w=[tk])
    for ht in range(DC):
        trk = T(lambda t, ht=ht: t.transpose(psT2[:, 0:32], kev[:, ht * 128:(ht + 1) * 128], cm.ident_bf[0:32, 0:32]),
                w=[kc2, ("chv", kb.now("chv"))])
        V(lambda v, ht=ht: v.tensor_copy(K32T[:, ht * 32:(ht + 1) * 32], psT2[:, 0:32]), w=[trk])
    kready = ("chv", kb.now("chv"))
    V(lambda v: v.memset(zbf[:], 0.0))

    identm = kb.sbuf(u + "idm", [128, 128], I32)
    kb.op("gpsimd", lambda g: g.iota(w2x2[:, 0:128].bitcast(F32) if False else zf[:, 0:64], pattern=[[1, 64]], base=0,
                                     channel_multiplier=0, allow_small_or_imprecise_dtypes=True),
          wait=[("chv", kb.now("chv"))], inc=[("chp", 1)])
    kb.op("gpsimd", lambda g: g.iota(identm[:].bitcast(F32), pattern=[[1, 128]], base=0,
                                     channel_multiplier=0, allow_small_or_imprecise_dtypes=True),
          inc=[("chp", 1)])
    ii = ("chp", kb.now("chp"))
    rowf = kb.sbuf(u + "rowf", [128, 1], F32)
    kb.op("gpsimd", lambda g: g.iota(rowf[:], pattern=[[0, 1]], base=0, channel_multiplier=1,
                                     allow_small_or_imprecise_dtypes=True), inc=[("chp", 1)])
    ri = ("chp", kb.now("chp"))
    V(lambda v: v.tensor_scalar(identm[:], identm[:].bitcast(F32), rowf[:], None, ALU.is_equal), w=[ii, ri])
    idmready = ("chv", kb.now("chv"))

    xnfm_v = xn_fm.rearrange("(c p) t -> p c t", p=128)
    ssfm_v = ss_out_fm.rearrange("(c p) t -> p c t", p=128)
    xt_ = kb.sbuf(u + "xt", [128, 4 * D], BF16)
    xt32 = kb.sbuf(u + "xt32", [32, 16 * D], BF16)
    Y32 = kb.sbuf(u + "Y32", [128, JS * D], BF16)
    Gst = [kb.sbuf(u + f"G{b}", [128, D], F32) for b in range(NSEQ)]
    Y5 = kb.sbuf(u + "Y5", [128, D], BF16)
    ctm_all = kb.sbuf(u + "ctm", [128, 32 * D], BF16)

    for b in range(NSEQ):
        V(lambda v, b=b: v.memset(Gst[b][:], 0.0))
    g0 = ("chv", kb.now("chv"))
    setup_done = [kready, idmready, actready, abready, mbready, g0,
                  ("dq_gpsimd", kb.now("dq_gpsimd"))]

    for b in range(NSEQ):
        for c5 in range(NC512):
            it = b * NC512 + c5
            tok0 = b * L + c5 * C
            prev_pe = ("pe", kb.now("pe"))
            ldt = kb.dma("sync", xt_[:].rearrange("p (k d) -> p k d", d=D),
                         xn_tm[tok0:tok0 + C, :].rearrange("(k p) d -> p k d", p=128),
                         wait=[("chv", kb.now("chv")), prev_pe] + (setup_done if it == 0 else []))
            ldt32 = kb.dma("sync", xt32[:].rearrange("p (k d) -> p k d", d=D),
                           xn_tm[tok0:tok0 + C, :].rearrange("(k p) d -> p k d", p=32),
                           wait=[("chv", kb.now("chv")), prev_pe])
            for j in range(JS):
                T(lambda t, j=j: t.matmul(
                    psS[:, 0:512], A32[:], xt32[:, j * D:j * D + 512],
                    start=True, stop=True),
                    w=[ldt32, ("chv", kb.now("chv"))])
                tS = T(lambda t, j=j: t.matmul(
                    psS[:, 512:1024], A32[:], xt32[:, j * D + 512:(j + 1) * D],
                    start=True, stop=True))
                V(lambda v, j=j: v.tensor_tensor(Y32[:, j * D:(j + 1) * D], psS[:], w2x2[:], ALU.mult), w=[tS])
            y32done = ("chv", kb.now("chv"))
            if c5 > 0:
                wY5 = V(lambda v, b=b: v.tensor_tensor(Y5[:], Gst[b][:], w2x2[:], ALU.mult))
            for tb in range(4):
                evac_prev = ("chv", kb.now("chv"))
                T(lambda t: t.matmul(psCt[:, 0:512], cm.ones_bf_row[:], zrow[:, 0:512],
                                     start=True, stop=False), w=[evac_prev])
                T(lambda t: t.matmul(psCt[:, 512:1024], cm.ones_bf_row[:], zrow[:, 512:1024],
                                     start=True, stop=False))
                first = False
                if c5 > 0:
                    T(lambda t, tb=tb: t.matmul(psCt[:, 0:512], MB[:, tb * 128 + 1: tb * 128 + 129], Y5[:, 0:512],
                                                start=False, stop=False), w=[wY5])
                    T(lambda t, tb=tb: t.matmul(psCt[:, 512:1024], MB[:, tb * 128 + 1: tb * 128 + 129], Y5[:, 512:1024],
                                                start=False, stop=False))
                for j in range(JS):
                    j0 = max(j + 1, tb * 4)
                    j1 = tb * 4 + 4
                    if j0 >= j1:
                        continue
                    p0 = 32 * (j0 - j - 1) + 1
                    ncols = 32 * (j1 - j0)
                    mo = (j0 - tb * 4) * 32
                    if mo == 96:
                        pieces = [("MBZ", 64, 64)]
                    elif mo == 32:
                        pieces = [("MB", 32, 32), ("MB", 64, 64)]
                    else:
                        pieces = [("MB", mo, ncols)]
                    for tab, mstart, mlen in pieces:
                        cp0 = p0 + (mstart - mo) if tab == "MB" else 0
                        for hf in (0, 512):
                            T(lambda t, j=j, tab=tab, mstart=mstart, mlen=mlen, cp0=cp0, first=first, hf=hf: t.matmul(
                                psCt[mstart:mstart + mlen, hf:hf + 512],
                                (MBZ[:] if tab == "MBZ" else MB[:, cp0:cp0 + mlen]),
                                Y32[:, j * D + hf:j * D + hf + 512],
                                start=False, stop=False),
                                w=[y32done])
                    first = False
                T(lambda t: t.matmul(psCt[:, 0:512], cm.ones_bf_row[:], zrow[:, 0:512],
                                     start=False, stop=True))
                T(lambda t: t.matmul(psCt[:, 512:1024], cm.ones_bf_row[:], zrow[:, 512:1024],
                                     start=False, stop=True))
                ctd = ("pe", kb.now("pe"))
                V(lambda v, it=it, tb=tb: v.tensor_copy(ctm_all[:, (it * 4 + tb) * D:(it * 4 + tb + 1) * D], psCt[:]), w=[ctd])
            carry_done = ("chv", kb.now("chv"))
            for half in (0, 512):
                T(lambda t, b=b, half=half: t.matmul(psS[:, half:half + 512], ACT_[:], Gst[b][:, half:half + 512],
                                                     start=True, stop=False), w=[carry_done])
                for k in range(4):
                    T(lambda t, k=k, half=half: t.matmul(
                        psS[:, half:half + 512], A512[:, 128 * k:128 * (k + 1)],
                        xt_[:, k * D + half:k * D + half + 512], start=False, stop=(k == 3)),
                        w=[ldt])
            gu = ("pe", kb.now("pe"))
            V(lambda v, b=b: v.tensor_copy(Gst[b][:], psS[:]), w=[gu])

    xf = kb.sbuf(u + "xf", [128, C], BF16)
    ssf = kb.sbuf(u + "ssf", [128, C], BF16)
    ctmp = kb.sbuf(u + "ctmp", [128, C], BF16)
    passA_done = ("chv", kb.now("chv"))
    for ht in range(DC):
        dstart = ("pe", kb.now("pe"))
        for d_ in range(32):
            V(lambda v, d_=d_, ht=ht: v.select(
                diags[:, d_ * 128:(d_ + 1) * 128], identm[:],
                K32T[:, ht * 32 + d_:ht * 32 + d_ + 1].to_broadcast([128, 128]), zbf[:]),
              w=[kready, idmready, dstart] if d_ == 0 else [])
        dgd = ("chv", kb.now("chv"))
        for b in range(NSEQ):
            for c5 in range(NC512):
                it = b * NC512 + c5
                tok0 = b * L + c5 * C
                ldf = kb.dma("sync", xf[:], xn_fm[128 * ht:128 * (ht + 1), tok0:tok0 + C],
                             wait=[("chv", kb.now("chv"))])
                for tb in range(4):
                    T(lambda t, it=it, tb=tb, ht=ht: t.transpose(
                        psTb[:, tb * 128:(tb + 1) * 128],
                        ctm_all[:, (it * 4 + tb) * D + ht * 128:(it * 4 + tb) * D + (ht + 1) * 128],
                        cm.ident_bf[:]),
                        w=[passA_done, dgd, ("chv", kb.now("chv"))] if tb == 0 else [])
                xr = xf[:].rearrange("p (blk s) -> p blk s", s=32)
                on = psN[:].rearrange("p (blk s) -> p blk s", s=32)
                for d_ in range(32):
                    T(lambda t, d_=d_, xr=xr, on=on: t.matmul(
                        on[:, :, d_:32] if d_ > 0 else psN[:].rearrange("p (blk s) -> p blk s", s=32)[:, :, 0:32],
                        diags[:, d_ * 128:(d_ + 1) * 128],
                        xr[:, :, 0:32 - d_], start=(d_ == 0), stop=(d_ == 31)),
                        w=[ldf] if d_ == 0 else [])
                nd = ("pe", kb.now("pe"))
                war = ("dq_gpsimd", kb.now("dq_gpsimd"))
                V(lambda v: v.tensor_copy(ctmp[:], psTb[:]), w=[nd, war])
                V(lambda v: v.tensor_tensor(ssf[:], psN[:], ctmp[:], ALU.add))
                kb.dma("gpsimd", ss_out_fm[128 * ht:128 * (ht + 1), tok0:tok0 + C], ssf[:],
                       wait=[("chv", kb.now("chv"))])
    return dict(K32T=K32T, P=P, MB=MB, PB=PB, A32=A32, A512=A512, ACT_=ACT_, diags=diags,
                ctm_all=ctm_all, Y32=Y32, w2x2=w2x2, identm=identm, Gst=Gst)


def emit_l0attn(kb: KB, cm: Common, xn_fm, conv_fm, ssWT, ss_b, gateWT, gate_b,
                finWT, fin_b, s_out_fm, tag="la"):
    u = kb.uniq(tag)
    T_ = 512
    ntiles = NTOK // T_

    wss = kb.sbuf(u + "wss", [128, DC * D], BF16)
    wgt = kb.sbuf(u + "wgt", [128, DC * D], BF16)
    wfn = kb.sbuf(u + "wfn", [128, DC * D], BF16)
    bss = kb.sbuf(u + "bss", [128, DC], F32)
    bgt = kb.sbuf(u + "bgt", [128, DC], F32)
    bfn = kb.sbuf(u + "bfn", [128, DC], F32)
    xnt = kb.sbuf(u + "xnt", [128, DC * T_], BF16)
    cvt = kb.sbuf(u + "cvt", [128, DC * T_], BF16)
    g1 = kb.sbuf(u + "g1", [128, DC * T_], BF16)
    g2 = kb.sbuf(u + "g2", [128, T_], BF16)
    pr = kb.sbuf(u + "pr", [128, DC * T_], BF16)
    sf = kb.sbuf(u + "sf", [128, DC * T_], F32)
    ps1 = kb.psum(u + "ps1", [128, T_], F32)
    ps2 = kb.psum(u + "ps2", [128, T_], F32)

    for w_ap, dst in ((ssWT, wss), (gateWT, wgt), (finWT, wfn)):
        kb.dma("gpsimd", dst[:].rearrange("p (c d) -> p c d", d=D),
               w_ap.rearrange("(c p) d -> p c d", p=128))
    for b_ap, dst in ((ss_b, bss), (gate_b, bgt), (fin_b, bfn)):
        kb.dma("sync", dst[:], b_ap)
    wld = ("dq_gpsimd", kb.now("dq_gpsimd"))
    bld = ("dq_sync", kb.now("dq_sync"))

    for i in range(ntiles):
        lx = kb.dma("sync", xnt[:].rearrange("p (c t) -> p c t", t=T_),
                    xn_fm.rearrange("(c p) t -> p c t", p=128)[:, :, i * T_:(i + 1) * T_],
                    wait=[("chs", kb.now("chs")), ("pe", kb.now("pe")), ("chv", kb.now("chv"))])
        lc = kb.dma("sync", cvt[:].rearrange("p (c t) -> p c t", t=T_),
                    conv_fm.rearrange("(c p) t -> p c t", p=128)[:, :, i * T_:(i + 1) * T_])
        for o in range(DC):
            for c in range(DC):
                kb.op("tensor", lambda t, o=o, c=c: t.matmul(
                    ps1[:], wgt[:, (c * DC + o) * 128:(c * DC + o + 1) * 128],
                    xnt[:, c * T_:(c + 1) * T_], start=(c == 0), stop=(c == DC - 1)),
                    wait=[lx, wld, ("chs", kb.now("chs"))] if c == 0 else (),
                    inc=[("pe", 1)] if c == DC - 1 else ())
            p1 = ("pe", kb.now("pe"))
            for c in range(DC):
                kb.op("tensor", lambda t, o=o, c=c: t.matmul(
                    ps2[:], wss[:, (c * DC + o) * 128:(c * DC + o + 1) * 128],
                    cvt[:, c * T_:(c + 1) * T_], start=(c == 0), stop=(c == DC - 1)),
                    wait=[lc, wld] if c == 0 else (),
                    inc=[("pe", 1)] if c == DC - 1 else ())
            p2 = ("pe", kb.now("pe"))
            kb.op("scalar", lambda s, o=o: s.activation(g1[:, o * T_:(o + 1) * T_], ps1[:], AF.Gelu,
                                                        bias=bgt[:, o:o + 1]),
                  wait=[p1, bld, ("chs", kb.now("chs"))], inc=[("chs", 1)])
            kb.op("scalar", lambda s, o=o: s.activation(g2[:], ps2[:], AF.Gelu, bias=bss[:, o:o + 1]),
                  wait=[p2, ("chs", kb.now("chs")), ("chv", kb.now("chv"))], inc=[("chs", 1)])
            kb.op("vector", lambda v, o=o: v.tensor_tensor(pr[:, o * T_:(o + 1) * T_],
                                                           g1[:, o * T_:(o + 1) * T_], g2[:], ALU.mult),
                  wait=[("chs", kb.now("chs")), ("chv", kb.now("chv"))], inc=[("chv", 1)])
        prd = ("chv", kb.now("chv"))
        for o in range(DC):
            for c in range(DC):
                kb.op("tensor", lambda t, o=o, c=c: t.matmul(
                    ps1[:], wfn[:, (c * DC + o) * 128:(c * DC + o + 1) * 128],
                    pr[:, c * T_:(c + 1) * T_], start=(c == 0), stop=(c == DC - 1)),
                    wait=[prd, ("chv", kb.now("chv"))] if c == 0 else (),
                    inc=[("pe", 1)] if c == DC - 1 else ())
            pf = ("pe", kb.now("pe"))
            kb.op("vector", lambda v, o=o: v.tensor_scalar(sf[:, o * T_:(o + 1) * T_], ps1[:],
                                                            bfn[:, o:o + 1], None, ALU.add),
                  wait=[pf, ("chv", kb.now("chv")), ("dq_gpsimd", kb.now("dq_gpsimd"))], inc=[("chv", 1)])
        sfd = ("chv", kb.now("chv"))
        kb.dma("gpsimd", s_out_fm.rearrange("(c p) t -> p c t", p=128)[:, :, i * T_:(i + 1) * T_],
               sf[:].rearrange("p (c t) -> p c t", t=T_), wait=[sfd])
    return


def emit_quant(kb: KB, out_bf, in_f32, q, first_waits=()):
    W = in_f32.shape[-1]
    NB = W // 16
    t1, t2, t3, t4 = q["t1"], q["t2"], q["t3"], q["t4"]
    m16, r16, iot = q["m16"], q["r16"], q["iot"]

    def V(fn, w=()):
        kb.op("vector", fn, wait=[("chv", kb.now("chv"))] + list(w), inc=[("chv", 1)])
        return ("chv", kb.now("chv"))

    MASKHI = int(np.array(0xFFC00000, np.uint32).view(np.int32))
    M22 = float(1.5 * 2.0 ** 22)
    a1 = t1[:, 0:W].rearrange("p (b s) -> p b s", s=16)
    V(lambda v: v.tensor_scalar(t1[:, 0:W].bitcast(I32), in_f32.bitcast(I32), 0x7FFFFFFF, None, ALU.bitwise_and),
      w=list(first_waits))
    V(lambda v: v.tensor_tensor(a1[:, :, 0:8], a1[:, :, 0:8], a1[:, :, 8:16], ALU.max))
    V(lambda v: v.tensor_tensor(a1[:, :, 0:4], a1[:, :, 0:4], a1[:, :, 4:8], ALU.max))
    V(lambda v: v.tensor_tensor(a1[:, :, 0:2], a1[:, :, 0:2], a1[:, :, 2:4], ALU.max))
    V(lambda v: v.tensor_tensor(m16[:, 0:NB], a1[:, :, 0:1].squeeze(2), a1[:, :, 1:2].squeeze(2), ALU.max))
    V(lambda v: v.tensor_scalar(m16[:, 0:NB], m16[:, 0:NB], float(np.float32(1.0) / np.float32(6.0)), 1e-12, ALU.mult, ALU.max))
    V(lambda v: v.reciprocal(r16[:, 0:NB], m16[:, 0:NB]))
    rb = r16[:, 0:NB].unsqueeze(2).to_broadcast([128, NB, 16])
    inb = in_f32.rearrange("p (b s) -> p b s", s=16)
    t2v = t2[:, 0:W].rearrange("p (b s) -> p b s", s=16)
    V(lambda v: v.tensor_tensor(t2v, inb, rb, ALU.mult))
    V(lambda v: v.tensor_scalar(t3[:, 0:W].bitcast(I32), t2[:, 0:W].bitcast(I32), 0x00200000, None, ALU.add))
    V(lambda v: v.tensor_scalar(t3[:, 0:W].bitcast(I32), t3[:, 0:W].bitcast(I32), MASKHI, None, ALU.bitwise_and))
    V(lambda v: v.tensor_scalar(t3[:, 0:W], t3[:, 0:W], 6.0, -6.0, ALU.min, ALU.max))
    V(lambda v: v.tensor_scalar(t4[:, 0:W], t2[:, 0:W], M22, M22, ALU.add, ALU.subtract))
    V(lambda v: v.tensor_scalar(t4[:, 0:W], t4[:, 0:W], 1.0, -1.0, ALU.min, ALU.max))
    V(lambda v: v.tensor_scalar(iot[:, 0:W].bitcast(I32) if iot.dtype != I32 else iot[:, 0:W], t2[:, 0:W].bitcast(I32), 0x7FFFFFFF, None, ALU.bitwise_and))
    V(lambda v: v.tensor_scalar(iot[:, 0:W], iot[:, 0:W], 0x3F800000, None, ALU.is_lt))
    V(lambda v: v.select(t2[:, 0:W], iot[:, 0:W], t4[:, 0:W], t3[:, 0:W]))
    sb = m16[:, 0:NB].unsqueeze(2).to_broadcast([128, NB, 16])
    outv = out_bf.rearrange("p (b s) -> p b s", s=16)
    return V(lambda v: v.tensor_tensor(outv, t2v, sb, ALU.mult))


RSPLIT = [0, 384, 768, 1152, 1536, 1920, 2304, 2560, 2816]
AGW = 384
HSLICES = [(0, 512), (512, 1024), (1024, 1536), (1536, 2048), (2048, 2560), (2560, 2816)]

def emit_l0ffn(kb: KB, cm: Common, xn_tm, xn_tm_lo, wg_slice, wu_slice, wd_slice,
               g_out_fm, tag="lf"):
    u = kb.uniq(tag)
    IDIM = 2816
    NI = IDIM // 128

    def V(fn, w=()):
        kb.op("vector", fn, wait=[("chv", kb.now("chv"))] + list(w), inc=[("chv", 1)])
        return ("chv", kb.now("chv"))

    def SC(fn, w=()):
        kb.op("scalar", fn, wait=[("chs", kb.now("chs"))] + list(w), inc=[("chs", 1)])
        return ("chs", kb.now("chs"))

    def T(fn, w=(), inc=True):
        kb.op("tensor", fn, wait=list(w), inc=[("pe", 1)] if inc else ())
        return ("pe", kb.now("pe"))

    q = dict(t1=kb.sbuf(u + "q1", [128, 512], F32), t2=kb.sbuf(u + "q2", [128, 512], F32),
             t3=kb.sbuf(u + "q3", [128, 512], F32), t4=kb.sbuf(u + "q4", [128, 512], F32),
             m16=kb.sbuf(u + "qm", [128, 32], F32), r16=kb.sbuf(u + "qr", [128, 32], F32),
             iot=kb.sbuf(u + "qi", [128, 512], I32))
    wf32 = kb.sbuf(u + "wf", [128, IDIM], F32)
    wqb = kb.sbuf(u + "wqb", [128, IDIM], BF16)
    wtr = kb.sbuf(u + "wtr", [128, DC * AGW], BF16)
    psT = kb.psum(u + "psT", [128, 128], BF16)
    psA = kb.psum(u + "psA", [128, AGW], F32)
    psB = kb.psum(u + "psB", [128, AGW], F32)
    psD = kb.psum(u + "psD", [128, 512], F32)

    ag_g_in = kb.dram(u + "aggi", [D, AGW], BF16)
    ag_g_out = kb.dram(u + "aggo", [8 * D, AGW], BF16)
    ag_u_in = kb.dram(u + "agui", [D, AGW], BF16)
    ag_u_out = kb.dram(u + "aguo", [8 * D, AGW], BF16)
    ag_d_in = kb.dram(u + "agdi", [IDIM, 128], BF16)
    ag_d_out = kb.dram(u + "agdo", [8 * IDIM, 128], BF16)

    def quant_w_gateup(w_dram, ag_in, ag_out):
        for s in range(3):
            ld = kb.dma("sync", wf32[:, 0:D], w_dram[s * 128:(s + 1) * 128, :],
                        wait=[("chv", kb.now("chv"))])
            for sl in range(2):
                emit_quant(kb, wqb[:, sl * 512:(sl + 1) * 512], wf32[:, sl * 512:(sl + 1) * 512], q,
                           first_waits=[ld])
            qd = ("chv", kb.now("chv"))
            for kc in range(DC):
                tt = T(lambda t, kc=kc: t.transpose(psT[:], wqb[:, kc * 128:(kc + 1) * 128], cm.ident_bf[:]),
                       w=[qd, ("chv", kb.now("chv"))])
                V(lambda v, kc=kc, s=s: v.tensor_copy(wtr[:, kc * AGW + s * 128:kc * AGW + (s + 1) * 128], psT[:]),
                  w=[tt, ("dq_gpsimd", kb.now("dq_gpsimd"))])
        wd_ = ("chv", kb.now("chv"))
        kb.dma("gpsimd", ag_in.rearrange("(kc p) w -> p kc w", p=128),
               wtr[:].rearrange("p (kc w) -> p kc w", w=AGW), wait=[wd_])
        kb.op("gpsimd", lambda g, ag_in=ag_in, ag_out=ag_out: g.collective_compute(
            "AllGather", ALU.bypass, replica_groups=[[0, 1, 2, 3, 4, 5, 6, 7]],
            ins=[ag_in.opt()], outs=[ag_out.opt()]),
            wait=[("dq_gpsimd", kb.now("dq_gpsimd"))], inc=[("cc", 1)])

    quant_w_gateup(wg_slice, ag_g_in, ag_g_out)
    quant_w_gateup(wu_slice, ag_u_in, ag_u_out)
    ldd = kb.dma("sync", wf32[:], wd_slice, wait=[("chv", kb.now("chv"))])
    for lo, hi in HSLICES:
        emit_quant(kb, wqb[:, lo:hi], wf32[:, lo:hi], q, first_waits=[ldd])
    qd = ("chv", kb.now("chv"))
    for ic in range(NI):
        tt = T(lambda t, ic=ic: t.transpose(psT[:], wqb[:, ic * 128:(ic + 1) * 128], cm.ident_bf[:]),
               w=[qd, ("chv", kb.now("chv"))])
        V(lambda v, ic=ic: v.tensor_copy(wtr[:, ic * 128:(ic + 1) * 128], psT[:]),
          w=[tt, ("dq_gpsimd", kb.now("dq_gpsimd"))])
    kb.dma("gpsimd", ag_d_in.rearrange("(ic p) w -> p ic w", p=128),
           wtr[:, 0:NI * 128].rearrange("p (ic w) -> p ic w", w=128),
           wait=[("chv", kb.now("chv"))])
    kb.op("gpsimd", lambda g: g.collective_compute(
        "AllGather", ALU.bypass, replica_groups=[[0, 1, 2, 3, 4, 5, 6, 7]],
        ins=[ag_d_in.opt()], outs=[ag_d_out.opt()]),
        wait=[("dq_gpsimd", kb.now("dq_gpsimd"))], inc=[("cc", 1)])
    agd = ("cc", kb.now("cc"))

    wgs = kb.sbuf(u + "wgs", [128, DC * AGW], BF16)
    wus = kb.sbuf(u + "wus", [128, DC * AGW], BF16)
    wds = kb.sbuf(u + "wds", [128, NI * 128], BF16)
    xin = kb.sbuf(u + "xin", [128, D], F32)
    xlo = kb.sbuf(u + "xlo", [128, D], BF16)
    xhi = kb.sbuf(u + "xhi", [128, D], BF16)
    xq = kb.sbuf(u + "xq", [128, D], BF16)
    xqf = kb.sbuf(u + "xqf", [128, DC * 512], BF16)
    sA = kb.sbuf(u + "sA", [128, AGW], F32)
    h32 = kb.sbuf(u + "h32", [128, IDIM], F32)
    hq = kb.sbuf(u + "hq", [128, IDIM], BF16)
    hqf = kb.sbuf(u + "hqf", [128, NI * 512], BF16)
    gf = kb.sbuf(u + "gf", [128, 512], F32)

    NMAC = NTOK // 512
    for mt in range(NMAC):
        for st in range(4):
            tok = mt * 512 + st * 128
            l1 = kb.dma("sync", xhi[:], xn_tm[tok:tok + 128, :],
                        wait=[("chv", kb.now("chv")), ("pe", kb.now("pe"))])
            l2 = kb.dma("sync", xlo[:], xn_tm_lo[tok:tok + 128, :])
            V(lambda v: v.tensor_tensor(xin[:], xhi[:], xlo[:], ALU.add), w=[l2, l1])
            for sl in range(2):
                emit_quant(kb, xq[:, sl * 512:(sl + 1) * 512], xin[:, sl * 512:(sl + 1) * 512], q)
            qd = ("chv", kb.now("chv"))
            for kc in range(DC):
                tt = T(lambda t, kc=kc: t.transpose(psT[:], xq[:, kc * 128:(kc + 1) * 128], cm.ident_bf[:]),
                       w=[qd, ("chv", kb.now("chv"))])
                V(lambda v, kc=kc, st=st: v.tensor_copy(xqf[:, kc * 512 + st * 128:kc * 512 + (st + 1) * 128], psT[:]),
                  w=[tt])
        xqd = ("chv", kb.now("chv"))
        for ic in range(8):
            r0, r1 = RSPLIT[ic], RSPLIT[ic + 1]
            wdt = r1 - r0
            rank = ic if ic < 6 else ic
            lg = kb.dma("gpsimd", wgs[:, 0:DC * wdt].rearrange("p (kc w) -> p kc w", w=wdt),
                        ag_g_out.rearrange("(r kc p) w -> r p kc w", r=8, p=128)[ic, :, :, 0:wdt],
                        wait=[("cc", kb.now("cc")), ("pe", kb.now("pe"))])
            lu = kb.dma("gpsimd", wus[:, 0:DC * wdt].rearrange("p (kc w) -> p kc w", w=wdt),
                        ag_u_out.rearrange("(r kc p) w -> r p kc w", r=8, p=128)[ic, :, :, 0:wdt])
            for st in range(4):
                for kc in range(DC):
                    kb.op("tensor", lambda t, kc=kc, st=st, wdt=wdt: t.matmul(
                        psA[:, 0:wdt], xqf[:, kc * 512 + st * 128:kc * 512 + (st + 1) * 128],
                        wgs[:, kc * wdt:(kc + 1) * wdt], start=(kc == 0), stop=(kc == DC - 1)),
                        wait=[xqd, lg, lu, ("chs", kb.now("chs")), ("chv", kb.now("chv"))] if kc == 0 else (),
                        inc=[("pe", 1)] if kc == DC - 1 else ())
                pa = ("pe", kb.now("pe"))
                for kc in range(DC):
                    kb.op("tensor", lambda t, kc=kc, st=st, wdt=wdt: t.matmul(
                        psB[:, 0:wdt], xqf[:, kc * 512 + st * 128:kc * 512 + (st + 1) * 128],
                        wus[:, kc * wdt:(kc + 1) * wdt], start=(kc == 0), stop=(kc == DC - 1)),
                        inc=[("pe", 1)] if kc == DC - 1 else ())
                pb = ("pe", kb.now("pe"))
                SC(lambda s, wdt=wdt: s.activation(sA[:, 0:wdt], psA[:, 0:wdt], AF.Silu), w=[pa])
                hdone = V(lambda v, r0=r0, wdt=wdt, st=st: v.tensor_tensor(
                    h32[:, r0:r0 + wdt], sA[:, 0:wdt], psB[:, 0:wdt], ALU.mult),
                    w=[pb, ("chs", kb.now("chs"))])
                for qlo in range(r0, r1, 512):
                    qhi = min(qlo + 512, r1)
                    emit_quant(kb, hq[:, qlo:qhi], h32[:, qlo:qhi], q, first_waits=[hdone])
                qd2 = ("chv", kb.now("chv"))
                for t128 in range(r0 // 128, r1 // 128):
                    tt = T(lambda t, t128=t128: t.transpose(psT[:], hq[:, t128 * 128:(t128 + 1) * 128], cm.ident_bf[:]),
                           w=[qd2, ("chv", kb.now("chv"))])
                    V(lambda v, t128=t128, st=st: v.tensor_copy(
                        hqf[:, t128 * 512 + st * 128:t128 * 512 + (st + 1) * 128], psT[:]), w=[tt])
        hqd = ("chv", kb.now("chv"))
        for oc in range(DC):
            lw = kb.dma("gpsimd", wds[:].rearrange("p (ic w) -> p ic w", w=128),
                        ag_d_out.rearrange("(r ic p) w -> r p ic w", r=8, p=128)[oc, :, :, :],
                        wait=[agd, ("pe", kb.now("pe"))])
            for ic in range(NI):
                kb.op("tensor", lambda t, ic=ic: t.matmul(
                    psD[:], wds[:, ic * 128:(ic + 1) * 128], hqf[:, ic * 512:(ic + 1) * 512],
                    start=(ic == 0), stop=(ic == NI - 1)),
                    wait=[hqd, lw, ("chs", kb.now("chs"))] if ic == 0 else (),
                    inc=[("pe", 1)] if ic == NI - 1 else ())
            pd = ("pe", kb.now("pe"))
            sg = SC(lambda s: s.activation(gf[:], psD[:], AF.Silu),
                    w=[pd, ("dq_gpsimd", kb.now("dq_gpsimd"))])
            kb.dma("gpsimd", g_out_fm[oc * 128:(oc + 1) * 128, mt * 512:(mt + 1) * 512], gf[:],
                   wait=[sg])
    return


def emit_mid(kb: KB, cm: Common, x_fm, s_fm, g_fm, x1_fm, xn1_fm, xnm_tm, logits_tm,
             nw_tile, rwn_dram, rb_row, tag="md"):
    u = kb.uniq(tag)
    T_ = 512
    E = 16
    ntiles = NTOK // T_

    def V(fn, w=()):
        kb.op("vector", fn, wait=[("chv", kb.now("chv"))] + list(w), inc=[("chv", 1)])
        return ("chv", kb.now("chv"))

    def SC(fn, w=()):
        kb.op("scalar", fn, wait=[("chs", kb.now("chs"))] + list(w), inc=[("chs", 1)])
        return ("chs", kb.now("chs"))

    def T(fn, w=(), inc=True):
        kb.op("tensor", fn, wait=list(w), inc=[("pe", 1)] if inc else ())
        return ("pe", kb.now("pe"))

    xt = kb.sbuf(u + "xt", [128, DC * T_], F32)
    st_ = kb.sbuf(u + "st", [128, T_], F32)
    gt_ = kb.sbuf(u + "gt", [128, T_], F32)
    sq = kb.sbuf(u + "sq", [128, T_], F32)
    xnf = kb.sbuf(u + "xnf", [128, DC * T_], BF16)
    xm32a = kb.sbuf(u + "xm32a", [128, DC * T_], F32)
    xmb = kb.sbuf(u + "xmb", [128, DC * T_], BF16)
    xnt = kb.sbuf(u + "xnt", [128, DC * 128], BF16)
    nw = kb.sbuf(u + "nw", [128, DC], F32)
    rwn = kb.sbuf(u + "rwn", [128, DC * E], F32)
    bbc = kb.sbuf(u + "bbc", [128, E], F32)
    brow = kb.sbuf(u + "brow", [1, E], F32)
    mrow = kb.sbuf(u + "mrow", [1, T_], F32)
    vrow = kb.sbuf(u + "vrow", [1, T_], F32)
    rrow = kb.sbuf(u + "rrow", [1, T_], F32)
    lrow = kb.sbuf(u + "lrow", [1, T_], F32)
    qrow = kb.sbuf(u + "qrow", [1, T_], F32)
    tf = kb.sbuf(u + "tf", [128, T_], F32)
    x32 = kb.sbuf(u + "x32", [128, T_], F32)
    lgt = kb.sbuf(u + "lgt", [128, E], F32)
    psA = kb.psum(u + "psA", [1, T_], F32)
    psB = kb.psum(u + "psB", [1, T_], F32)
    psM = kb.psum(u + "psM", [128, T_], F32)
    psR = kb.psum(u + "psR", [128, T_], F32)
    psQ = kb.psum(u + "psQ", [128, T_], F32)
    psL = kb.psum(u + "psL", [128, E], F32)
    psT = kb.psum(u + "psT", [128, 128], BF16)

    x_v = x_fm.rearrange("(c p) t -> p c t", p=128)
    s_v = s_fm.rearrange("(c p) t -> p c t", p=128)
    g_v = g_fm.rearrange("(c p) t -> p c t", p=128)
    x1_v = x1_fm.rearrange("(c p) t -> p c t", p=128)
    xnf_v = xn1_fm.rearrange("(c p) t -> p c t", p=128)

    l1 = kb.dma("sync", nw[:], nw_tile)
    l2 = kb.dma("sync", rwn[:].rearrange("p (c e) -> p c e", e=E),
                rwn_dram.rearrange("(c p) e -> p c e", p=128))
    l3 = kb.dma("sync", brow[:], rb_row)
    tb = T(lambda t: t.matmul(psL[:], cm.ones_f32_row[:], brow[:], start=True, stop=True), w=[l3, l1, l2])
    V(lambda v: v.tensor_copy(bbc[:], psL[:]), w=[tb])
    setup = ("chv", kb.now("chv"))

    EPS = 1e-5
    for i in range(ntiles):
        ld = kb.dma("sync", xt[:].rearrange("p (c t) -> p c t", t=T_), x_v[:, :, i * T_:(i + 1) * T_],
                    wait=[("chv", kb.now("chv")), ("pe", kb.now("pe")), ("chs", kb.now("chs")),
                          ("peB", kb.now("peB")), ("dq_gpsimd", kb.now("dq_gpsimd")), setup])
        for c in range(DC):
            l4 = kb.dma("sync", st_[:], s_v[:, c, i * T_:(i + 1) * T_],
                        wait=[("chv", kb.now("chv"))])
            l5 = kb.dma("sync", gt_[:], g_v[:, c, i * T_:(i + 1) * T_])
            V(lambda v, c=c: v.tensor_tensor(st_[:], st_[:], gt_[:], ALU.mult), w=[l4, l5])
            V(lambda v, c=c: v.tensor_tensor(xt[:, c * T_:(c + 1) * T_], xt[:, c * T_:(c + 1) * T_],
                                             st_[:], ALU.add), w=[ld])
        x1d = ("chv", kb.now("chv"))
        kb.dma("gpsimd", x1_v[:, :, i * T_:(i + 1) * T_], xt[:].rearrange("p (c t) -> p c t", t=T_),
               wait=[x1d])
        for c in range(DC):
            kb.op("tensor", lambda t, c=c: t.matmul(psA[:], cm.ones_f32_col[:], xt[:, c * T_:(c + 1) * T_],
                                                    start=(c == 0), stop=(c == DC - 1)),
                  wait=[x1d] if c == 0 else (), inc=[("pe", 1)] if c == DC - 1 else ())
        for c in range(DC):
            kb.op("scalar", lambda s, c=c: s.activation(sq[:], xt[:, c * T_:(c + 1) * T_], AF.Square),
                  wait=[x1d, ("chs", kb.now("chs")), ("peB", kb.now("peB"))], inc=[("chs", 1)])
            kb.op("tensor", lambda t, c=c: t.matmul(psB[:], cm.ones_f32_col[:], sq[:],
                                                    start=(c == 0), stop=(c == DC - 1)),
                  wait=[("chs", kb.now("chs"))], inc=[("peB", 1)])
        pe_d = ("pe", kb.now("pe")); peB_d = ("peB", kb.now("peB"))
        V(lambda v: v.tensor_scalar(mrow[:], psA[:], 1.0 / D, None, ALU.mult), w=[pe_d, peB_d])
        V(lambda v: v.tensor_scalar(vrow[:], psB[:], 1.0 / D, None, ALU.mult))
        V(lambda v: v.tensor_tensor(tf[0:1, :], mrow[:], mrow[:], ALU.mult))
        vd = V(lambda v: v.tensor_tensor(vrow[:], vrow[:], tf[0:1, :], ALU.subtract))
        s1 = SC(lambda s: s.activation(lrow[:], vrow[:], AF.Ln, bias=cm.eps1[:]), w=[vd])
        s2 = SC(lambda s: s.activation(rrow[:], lrow[:], AF.Exp, scale=-0.5))
        V(lambda v: v.tensor_tensor(tf[0:1, :], rrow[:], rrow[:], ALU.mult), w=[s2])
        vq = V(lambda v: v.tensor_tensor(tf[0:1, :], tf[0:1, :], vrow[:], ALU.mult))
        s3 = SC(lambda s: s.activation(lrow[:], tf[0:1, :], AF.Ln, bias=cm.eps1[:]), w=[vq])
        s4 = SC(lambda s: s.activation(qrow[:], lrow[:], AF.Exp, scale=-0.5))
        t1 = T(lambda t: t.matmul(psM[:], cm.ones_f32_row[:], mrow[:], start=True, stop=True), w=[vd])
        t2 = T(lambda t: t.matmul(psR[:], cm.ones_f32_row[:], rrow[:], start=True, stop=True), w=[s2])
        t3 = T(lambda t: t.matmul(psQ[:], cm.ones_f32_row[:], qrow[:], start=True, stop=True), w=[s4])
        bc = ("pe", kb.now("pe"))
        for c in range(DC):
            V(lambda v, c=c: v.tensor_tensor(tf[:], xt[:, c * T_:(c + 1) * T_], psM[:], ALU.subtract), w=[bc])
            V(lambda v, c=c: v.tensor_tensor(x32[:], tf[:], psR[:], ALU.mult))
            V(lambda v, c=c: v.tensor_copy(xnf[:, c * T_:(c + 1) * T_], x32[:]))
            V(lambda v, c=c: v.tensor_scalar(xm32a[:, c * T_:(c + 1) * T_], x32[:], nw[:, c:c + 1], None, ALU.mult))
            V(lambda v, c=c: v.tensor_tensor(xm32a[:, c * T_:(c + 1) * T_], xm32a[:, c * T_:(c + 1) * T_], psQ[:], ALU.mult))
            V(lambda v, c=c: v.tensor_copy(xmb[:, c * T_:(c + 1) * T_], xm32a[:, c * T_:(c + 1) * T_]))
        xmd = ("chv", kb.now("chv"))
        for stt_ in range(T_ // 128):
            for c in range(DC):
                kb.op("tensor", lambda t, c=c, stt_=stt_: t.matmul(
                    psL[:], xm32a[:, c * T_ + stt_ * 128:c * T_ + (stt_ + 1) * 128],
                    rwn[:, c * E:(c + 1) * E], start=(c == 0), stop=(c == DC - 1)),
                    wait=[xmd, ("chv", kb.now("chv"))] if c == 0 else (),
                    inc=[("pe", 1)] if c == DC - 1 else ())
            pl = ("pe", kb.now("pe"))
            lg = V(lambda v: v.tensor_tensor(lgt[:], psL[:], bbc[:], ALU.add),
                   w=[pl, ("dq_gpsimd", kb.now("dq_gpsimd"))])
            kb.dma("gpsimd", logits_tm[i * T_ + stt_ * 128:i * T_ + (stt_ + 1) * 128, :], lgt[:],
                   wait=[lg])
        kb.dma("gpsimd", xnf_v[:, :, i * T_:(i + 1) * T_], xnf[:].rearrange("p (c t) -> p c t", t=T_),
               wait=[xmd])
        for b in range(T_ // 128):
            war = ("dq_gpsimd", kb.now("dq_gpsimd"))
            for c in range(DC):
                tt = kb.op("tensor", lambda t, b=b, c=c: t.transpose(
                    psT[:], xmb[:, c * T_ + b * 128: c * T_ + (b + 1) * 128], cm.ident_bf[:]),
                    wait=[xmd, ("chv", kb.now("chv"))], inc=[("pe", 1)])
                kb.op("vector", lambda v, c=c: v.tensor_copy(xnt[:, c * 128:(c + 1) * 128], psT[:]),
                      wait=[("pe", kb.now("pe")), ("chv", kb.now("chv")), war], inc=[("chv", 1)])
            kb.dma("gpsimd", xnm_tm[i * T_ + b * 128:i * T_ + (b + 1) * 128, :], xnt[:],
                   wait=[("chv", kb.now("chv"))])
    return


def emit_attn(kb: KB, cm: Common, xn1_fm, wqkvT, bq_t, bk_t, bv_row, woutT, bo_t,
              s1_fm, tag="at"):
    u = kb.uniq(tag)
    L = 2048
    NH = 16
    QT = 256

    def V(fn, w=()):
        kb.op("vector", fn, wait=[("chv", kb.now("chv"))] + list(w), inc=[("chv", 1)])
        return ("chv", kb.now("chv"))

    def SC(fn, w=()):
        kb.op("scalar", fn, wait=[("chs", kb.now("chs"))] + list(w), inc=[("chs", 1)])
        return ("chs", kb.now("chs"))

    def T(fn, w=(), inc=True):
        kb.op("tensor", fn, wait=list(w), inc=[("pe", 1)] if inc else ())
        return ("pe", kb.now("pe"))

    wq = kb.sbuf(u + "wq", [128, DC * D], BF16)
    qf = kb.sbuf(u + "qf", [128, DC * L], BF16)
    kf = kb.sbuf(u + "kf", [128, DC * L], BF16)
    vt = kb.sbuf(u + "vt", [128, (L // 128) * D], BF16)
    xnt = kb.sbuf(u + "xnt", [128, DC * L], BF16)
    ao = xnt
    bqs = kb.sbuf(u + "bqs", [128, DC], F32)
    bks = kb.sbuf(u + "bks", [128, DC], F32)
    bos = kb.sbuf(u + "bos", [128, DC], F32)
    bvr32 = kb.sbuf(u + "bvr32", [1, D], F32)
    bvr = kb.sbuf(u + "bvr", [1, D], BF16)
    onesc = kb.sbuf(u + "onesc", [128, 1], BF16)
    Bt = kb.sbuf(u + "Bt", [128, 3 * QT], BF16)
    iof = kb.sbuf(u + "iof", [128, QT], F32)
    iom = kb.sbuf(u + "iom", [128, QT], F32)
    ioe = kb.sbuf(u + "ioe", [128, QT], F32)
    Et = kb.sbuf(u + "Et", [128, QT], BF16)
    Et2s = [kb.sbuf(u + f"Et2{i}", [128, QT], BF16) for i in range(2)]
    rr = kb.sbuf(u + "rr", [128, QT], F32)
    rb = kb.sbuf(u + "rb", [128, QT], F32)
    sf = kb.sbuf(u + "sf", [128, 512], F32)
    psS = [kb.psum(u + f"psS{i}", [128, QT], F32) for i in range(2)]
    psO = kb.psum(u + "psO", [128, QT], F32)
    psDen = kb.psum(u + "psDen", [1, QT], F32)
    psB2 = kb.psum(u + "psB2", [128, QT], F32)
    psV = kb.psum(u + "psV", [128, 512], F32)

    for ci, off in enumerate((-128, 0, 128)):
        kb.op("gpsimd", lambda g, off=off: g.iota(iof[:], pattern=[[-1, QT]], base=off,
                                                  channel_multiplier=1,
                                                  allow_small_or_imprecise_dtypes=True),
              wait=[("chv", kb.now("chv")), ("chs", kb.now("chs"))], inc=[("chp", 1)])
        ip = ("chp", kb.now("chp"))
        mv = V(lambda v: v.tensor_scalar(iom[:], iof[:], 0.5, None, ALU.is_le), w=[ip])
        cl = V(lambda v: v.tensor_scalar(iof[:], iof[:], 0.0, None, ALU.min))
        sd = SC(lambda s: s.activation(ioe[:], iof[:], AF.Exp), w=[cl])
        V(lambda v, ci=ci: v.tensor_tensor(Bt[:, ci * QT:(ci + 1) * QT], ioe[:], iom[:], ALU.mult),
          w=[sd, mv])
    V(lambda v: v.memset(onesc[:], 1.0))
    l1 = kb.dma("sync", bqs[:], bq_t)
    l2 = kb.dma("sync", bks[:], bk_t)
    l3 = kb.dma("sync", bos[:], bo_t)
    l4 = kb.dma("sync", bvr32[:], bv_row)
    V(lambda v: v.tensor_copy(bvr[:], bvr32[:]), w=[l4, l1, l2, l3])
    setup = ("chv", kb.now("chv"))

    for sq in range(2):
        tok0 = sq * L
        lx = kb.dma("sync", xnt[:].rearrange("p (c t) -> p c t", t=L),
                    xn1_fm.rearrange("(c p) t -> p c t", p=128)[:, :, tok0:tok0 + L],
                    wait=[setup, ("chv", kb.now("chv")), ("pe", kb.now("pe")), ("chs", kb.now("chs"))])
        for grp, dst, bias, scale in ((0, qf, bqs, 0.125), (1, kf, bks, 1.0)):
            lw = kb.dma("gpsimd", wq[:].rearrange("p (c d) -> p c d", d=D),
                        wqkvT.rearrange("(c p) d -> p c d", p=128)[:, :, grp * D:(grp + 1) * D],
                        wait=[("pe", kb.now("pe"))])
            for o in range(DC):
                for tc in range(L // 512):
                    for c in range(DC):
                        kb.op("tensor", lambda t, o=o, tc=tc, c=c: t.matmul(
                            psV[:], wq[:, c * D + o * 128:c * D + (o + 1) * 128],
                            xnt[:, c * L + tc * 512:c * L + (tc + 1) * 512],
                            start=(c == 0), stop=(c == DC - 1)),
                            wait=[lx, lw, ("chv", kb.now("chv"))] if c == 0 else (),
                            inc=[("pe", 1)] if c == DC - 1 else ())
                    pq = ("pe", kb.now("pe"))
                    V(lambda v, o=o, tc=tc, dst=dst, bias=bias, scale=scale: v.tensor_scalar(
                        dst[:, o * L + tc * 512:o * L + (tc + 1) * 512], psV[:],
                        bias[:, o:o + 1], scale, ALU.add, ALU.mult), w=[pq])
        lw = kb.dma("gpsimd", wq[:].rearrange("p (c d) -> p c d", d=D),
                    wqkvT.rearrange("(c p) d -> p c d", p=128)[:, :, 2 * D:3 * D],
                    wait=[("pe", kb.now("pe"))])
        for tk in range(L // 128):
            for half in range(2):
                for c in range(DC):
                    kb.op("tensor", lambda t, tk=tk, half=half, c=c: t.matmul(
                        psV[:], xnt[:, c * L + tk * 128:c * L + (tk + 1) * 128],
                        wq[:, c * D + half * 512:c * D + (half + 1) * 512],
                        start=(c == 0), stop=False),
                        wait=[lw, ("chv", kb.now("chv"))] if c == 0 else (), inc=())
                kb.op("tensor", lambda t, half=half: t.matmul(
                    psV[:], cm.ones_bf_row[:], bvr[:, half * 512:(half + 1) * 512],
                    start=False, stop=True), inc=[("pe", 1)])
                pv = ("pe", kb.now("pe"))
                V(lambda v, tk=tk, half=half: v.tensor_copy(
                    vt[:, tk * D + half * 512:tk * D + (half + 1) * 512], psV[:]), w=[pv])
        qkvd = ("chv", kb.now("chv"))
        for h in range(NH):
            c = h // 2
            po = (h % 2) * 64
            for qi in range(L // QT):
                q0 = qi * QT
                kts = [k0 for k0 in (q0 - 128, q0, q0 + 128) if 0 <= k0 < L]
                par = 0
                for ik, k0 in enumerate(kts):
                    ci = (k0 - q0) // 128 + 1
                    first = ik == 0
                    last = ik == len(kts) - 1
                    ps = psS[par]
                    Et2 = Et2s[par]; par ^= 1
                    tq = T(lambda t, c=c, po=po, k0=k0, q0=q0, ps=ps: t.matmul(
                        ps[:], kf[po:po + 64, c * L + k0:c * L + k0 + 128],
                        qf[po:po + 64, c * L + q0:c * L + q0 + QT], start=True, stop=True),
                        w=[qkvd, ("chs", kb.now("chs")), ("chv", kb.now("chv"))])
                    se = SC(lambda s, ps=ps: s.activation(Et[:], ps[:], AF.Exp),
                            w=[tq, ("chv", kb.now("chv"))])
                    ve = V(lambda v, ci=ci, Et2=Et2: v.tensor_tensor(Et2[:], Et[:], Bt[:, ci * QT:(ci + 1) * QT], ALU.mult),
                           w=[se, ("pe", kb.now("pe"))])
                    T(lambda t, k0=k0, h=h, first=first, last=last, po=po, Et2=Et2: t.matmul(
                        psO[po:po + 64, :], vt[:, (k0 // 128) * D + h * 64:(k0 // 128) * D + h * 64 + 64],
                        Et2[:], start=first, stop=last), w=[ve], inc=False)
                    T(lambda t, first=first, last=last, Et2=Et2: t.matmul(
                        psDen[:], onesc[:], Et2[:], start=first, stop=last),
                        inc=(ik == len(kts) - 1))
                pvd = ("pe", kb.now("pe"))
                rv = V(lambda v: v.reciprocal(rr[0:1, :], psDen[:]), w=[pvd])
                tb2 = T(lambda t, po=po: t.matmul(psB2[po:po + 64, :], cm.ones_f32_row[:, 0:64], rr[0:1, :],
                                                  start=True, stop=True), w=[rv])
                cv = V(lambda v, po=po: v.tensor_copy(rb[po:po + 64, :], psB2[po:po + 64, :]), w=[tb2])
                V(lambda v, po=po, c=c, q0=q0: v.tensor_tensor(
                    ao[po:po + 64, c * L + q0:c * L + q0 + QT],
                    psO[po:po + 64, :], rb[po:po + 64, :], ALU.mult), w=[cv])
        attnd = ("chv", kb.now("chv"))
        lw2 = kb.dma("gpsimd", wq[:].rearrange("p (c d) -> p c d", d=D),
                     woutT.rearrange("(c p) d -> p c d", p=128),
                     wait=[("pe", kb.now("pe"))])
        for o in range(DC):
            for tc in range(L // 512):
                for c2 in range(DC):
                    kb.op("tensor", lambda t, o=o, tc=tc, c2=c2: t.matmul(
                        psV[:], wq[:, c2 * D + o * 128:c2 * D + (o + 1) * 128],
                        ao[:, c2 * L + tc * 512:c2 * L + (tc + 1) * 512],
                        start=(c2 == 0), stop=(c2 == DC - 1)),
                        wait=[attnd, lw2, ("chv", kb.now("chv"))] if c2 == 0 else (),
                        inc=[("pe", 1)] if c2 == DC - 1 else ())
                po_ = ("pe", kb.now("pe"))
                sv = V(lambda v, o=o: v.tensor_scalar(sf[:], psV[:], bos[:, o:o + 1], None, ALU.add),
                       w=[po_, ("dq_gpsimd", kb.now("dq_gpsimd"))])
                kb.dma("gpsimd", s1_fm[o * 128:(o + 1) * 128, tok0 + tc * 512:tok0 + (tc + 1) * 512],
                       sf[:], wait=[sv])
    return


def emit_moe(kb: KB, cm: Common, logits_tm, xnm_tm, coreid2_d, fc1T, fc1b, fc2T, fc2b,
             moe_fm, tag="mo"):
    from concourse.bass_core_rust import IndirectOffsetOnAxis
    u = kb.uniq(tag)
    E, CAP, FF = 16, 2048, 4096
    NTG = 8 * NTOK

    def V(fn, w=()):
        kb.op("vector", fn, wait=[("chv", kb.now("chv"))] + list(w), inc=[("chv", 1)])
        return ("chv", kb.now("chv"))

    def SC(fn, w=()):
        kb.op("scalar", fn, wait=[("chs", kb.now("chs"))] + list(w), inc=[("chs", 1)])
        return ("chs", kb.now("chs"))

    def T(fn, w=(), inc=True):
        kb.op("tensor", fn, wait=list(w), inc=[("pe", 1)] if inc else ())
        return ("pe", kb.now("pe"))

    def CC(kind, op, ins, outs, w=()):
        kb.op("gpsimd", lambda g: g.collective_compute(
            kind, op, replica_groups=[[0, 1, 2, 3, 4, 5, 6, 7]],
            ins=[a.opt() for a in ins], outs=[a.opt() for a in outs]),
            wait=list(w), inc=[("cc", 1)])
        return ("cc", kb.now("cc"))

    def IG(out, in_, idxcol, w=()):
        return kb.dma_raw("gpsimd", lambda g: g.indirect_dma_start(
            out, None, in_, IndirectOffsetOnAxis(idxcol, 0)), wait=w, sname="dq_ind")

    ag_lg_in = kb.dram(u + "algi", [E, NTOK], F32)
    LGT_D = kb.dram(u + "algo", [128, NTOK], F32)
    ag_xn_in = kb.dram(u + "axni", [NTOK, D], BF16)
    XNALL = kb.dram(u + "axno", [NTG, D], BF16)
    ag_lt_in = kb.dram(u + "alti", [NTOK, E], F32)
    LGALL = kb.dram(u + "alto", [NTG, E], F32)
    CUM2D = kb.dram(u + "cum2d", [128 * 32, 128], F32)
    BEXD = kb.dram(u + "bexd", [E * 256, 1], F32)
    IDXBD = kb.dram(u + "idxbd", [E, CAP], F32)
    CRIDD = kb.dram(u + "cridd", [E, CAP], F32)
    YLOC = kb.dram(u + "yloc", [2 * CAP, D], BF16)
    YALL = kb.dram(u + "yall", [8 * 2 * CAP + 128, D], BF16)

    lgs = kb.sbuf(u + "lgs", [128, NTOK], F32)
    mkg = kb.sbuf(u + "mkg", [128, NTOK], BF16)
    rstp = kb.sbuf(u + "rstp", [128, NTOK], BF16)
    zz16 = kb.sbuf(u + "zz16", [128, 256], F32)
    lo = kb.sbuf(u + "lo", [128, 1], F32)
    hi = kb.sbuf(u + "hi", [128, 1], F32)
    mid = kb.sbuf(u + "mid", [128, 1], F32)
    nmid = kb.sbuf(u + "nmid", [128, 1], F32)
    cnt = kb.sbuf(u + "cnt", [128, 1], F32)
    msk1 = kb.sbuf(u + "msk1", [128, 1], I32)
    SELE = kb.sbuf(u + "SELE", [128, 128], F32)
    SEL8 = kb.sbuf(u + "SEL8", [16, 128], F32)
    rm16 = kb.sbuf(u + "rm16", [128, 1], F32)
    rd8 = kb.sbuf(u + "rd8", [128, 1], F32)
    ra7 = kb.sbuf(u + "ra7", [128, 1], F32)
    riot = kb.sbuf(u + "riot", [128, 1], F32)
    fio = kb.sbuf(u + "fio", [128, 256], F32)
    bc = kb.sbuf(u + "bc", [128, 32], F32)
    BCE = kb.sbuf(u + "BCE", [16, 256], F32)
    BCUM = kb.sbuf(u + "BCUM", [16, 256], F32)
    BEX = kb.sbuf(u + "BEX", [16, 256], F32)
    BCX8 = kb.sbuf(u + "BCX8", [128, 256], F32)
    bof = kb.sbuf(u + "bof", [128, 256], F32)
    idxb = kb.sbuf(u + "idxb", [128, 256], F32)
    cridf = kb.sbuf(u + "cridf", [128, 256], F32)
    ii1 = kb.sbuf(u + "ii1", [128, 256], I32)
    ii2 = kb.sbuf(u + "ii2", [128, 256], I32)
    slotv = kb.sbuf(u + "slotv", [128, 256], F32)
    my2 = kb.sbuf(u + "my2", [2, 1], I32)
    my2f = kb.sbuf(u + "my2f", [2, 1], F32)
    my2r = kb.sbuf(u + "my2r", [1, 2], F32)
    e256c = kb.sbuf(u + "e256c", [128, 2], F32)
    myext = kb.sbuf(u + "myext", [2, 2 * CAP], F32)
    myIB = kb.sbuf(u + "myIB", [128, 32], F32)
    myCR = kb.sbuf(u + "myCR", [128, 32], F32)
    myIBi = kb.sbuf(u + "myIBi", [128, 32], I32)
    myCRi = kb.sbuf(u + "myCRi", [128, 32], I32)
    xei = kb.sbuf(u + "xei", [128, 32], I32)
    bexg = kb.sbuf(u + "bexg", [128, 32], F32)
    G2 = kb.sbuf(u + "G2", [128, 128], F32)
    sloc = kb.sbuf(u + "sloc", [128, 1], F32)
    r2c = kb.sbuf(u + "r2c", [128, 1], F32)
    posc = kb.sbuf(u + "posc", [128, 1], F32)
    tmpc = kb.sbuf(u + "tmpc", [128, 128], F32)
    ehot = kb.sbuf(u + "ehot", [128, E * 2], F32)
    lgrow = kb.sbuf(u + "lgrow", [128, E], F32)
    gprob = kb.sbuf(u + "gprob", [128, E], F32)
    gcol = kb.sbuf(u + "gcol", [128, 2 * 16], F32)
    psT = kb.psum(u + "psT", [128, 128], F32)
    psC = kb.psum(u + "psC", [128, 256], F32)
    psF = kb.psum(u + "psF", [128, 512], F32)
    psTb = kb.psum(u + "psTb", [128, 128], BF16)

    ltok = kb.sbuf(u + "ltok", [128, E], F32)
    lgTl = kb.sbuf(u + "lgTl", [16, NTOK], F32)
    for i in range(NTOK // 128):
        ll = kb.dma("sync", ltok[:], logits_tm[i * 128:(i + 1) * 128, :],
                    wait=[("chv", kb.now("chv")), ("pe", kb.now("pe"))])
        tt = T(lambda t: t.transpose(psT[0:16, :], ltok[:], cm.ident_f32[:]), w=[ll, ("chv", kb.now("chv"))])
        V(lambda v, i=i: v.tensor_copy(lgTl[:, i * 128:(i + 1) * 128], psT[0:16, :]), w=[tt])
    d1 = kb.dma("gpsimd", ag_lg_in, lgTl[:], wait=[("chv", kb.now("chv"))])
    cc1 = CC("AllGather", ALU.bypass, [ag_lg_in], [LGT_D], w=[("dq_gpsimd", kb.now("dq_gpsimd"))])
    d2 = kb.dma("gpsimd", ag_xn_in, xnm_tm)
    d3 = kb.dma("gpsimd", ag_lt_in, logits_tm)
    cc2 = CC("AllGather", ALU.bypass, [ag_xn_in], [XNALL], w=[("dq_gpsimd", kb.now("dq_gpsimd"))])
    cc3 = CC("AllGather", ALU.bypass, [ag_lt_in], [LGALL], w=[("dq_gpsimd", kb.now("dq_gpsimd"))])

    G = kb.op
    kb.op("gpsimd", lambda g: g.iota(riot[:], pattern=[[0, 1]], base=0, channel_multiplier=1,
                                     allow_small_or_imprecise_dtypes=True), inc=[("chp", 1)])
    kb.op("gpsimd", lambda g: g.iota(fio[:], pattern=[[1, 256]], base=0, channel_multiplier=0,
                                     allow_small_or_imprecise_dtypes=True), inc=[("chp", 1)])
    kb.op("gpsimd", lambda g: g.iota(tmpc[:], pattern=[[1, 128]], base=0, channel_multiplier=0,
                                     allow_small_or_imprecise_dtypes=True), inc=[("chp", 1)])
    iod = ("chp", kb.now("chp"))
    V(lambda v: v.tensor_scalar(msk1[:], riot[:], None, None, ALU.bypass) if False else
      v.tensor_copy(mid[:], riot[:]), w=[iod])
    V(lambda v: v.tensor_scalar(rm16[:].bitcast(I32), riot[:].bitcast(I32), 0, None, ALU.bitwise_and))
    riox = kb.sbuf(u + "riox", [128, 1], I32)
    kb.op("gpsimd", lambda g: g.iota(riox[:], pattern=[[0, 1]], base=0, channel_multiplier=1),
          inc=[("chp", 1)])
    io2 = ("chp", kb.now("chp"))
    V(lambda v: v.tensor_scalar(msk1[:], riox[:], 15, None, ALU.bitwise_and), w=[io2])
    V(lambda v: v.tensor_copy(rm16[:], msk1[:]))
    V(lambda v: v.tensor_scalar(msk1[:], riox[:], 3, None, ALU.arith_shift_right))
    V(lambda v: v.tensor_copy(rd8[:], msk1[:]))
    V(lambda v: v.tensor_scalar(msk1[:], riox[:], 7, None, ALU.bitwise_and))
    V(lambda v: v.tensor_scalar(msk1[:], msk1[:], 8, None, ALU.logical_shift_left))
    V(lambda v: v.tensor_copy(ra7[:], msk1[:]))
    V(lambda v: v.tensor_scalar(SELE[:].bitcast(I32), tmpc[:].bitcast(I32), 0, None, ALU.bitwise_and) if False else
      v.tensor_copy(SELE[:], tmpc[:]))
    ti = kb.sbuf(u + "ti", [128, 128], I32)
    V(lambda v: v.tensor_copy(ti[:], tmpc[:]))
    V(lambda v: v.tensor_scalar(ti[:], ti[:], 15, None, ALU.bitwise_and))
    V(lambda v: v.tensor_copy(SELE[:], ti[:]))
    V(lambda v: v.tensor_scalar(SELE[:], SELE[:], rm16[:], None, ALU.is_equal))
    V(lambda v: v.tensor_copy(ti[:], tmpc[:]))
    V(lambda v: v.tensor_scalar(ti[:], ti[:], 3, None, ALU.arith_shift_right))
    V(lambda v: v.tensor_copy(tmpc[:], ti[:]))
    V(lambda v: v.tensor_scalar(SEL8[:], tmpc[0:16, :], rm16[0:16, :], None, ALU.is_equal))
    V(lambda v: v.tensor_scalar(slotv[:], fio[:], ra7[:], None, ALU.add))
    sgn = mkg
    cum2 = lgs
    fi2 = lgs[0:128, :].bitcast(I32)
    kb.op("gpsimd", lambda g: g.iota(fi2, pattern=[[1, NTOK]], base=0, channel_multiplier=0),
          wait=[("chv", kb.now("chv"))], inc=[("chp", 1)])
    io3 = ("chp", kb.now("chp"))
    V(lambda v: v.tensor_scalar(fi2, fi2, 127, None, ALU.bitwise_and), w=[io3])
    V(lambda v: v.tensor_scalar(rstp[:], fi2, 0, None, ALU.is_gt))
    helpers = ("chv", kb.now("chv"))

    ld_lgs = kb.dma("sync", lgs[:], LGT_D, wait=[cc1, ("chv", kb.now("chv"))])
    V(lambda v: v.memset(lo[:], -64.0))
    V(lambda v: v.memset(hi[:], 64.0))
    for it in range(46):
        V(lambda v: v.tensor_tensor(mid[:], lo[:], hi[:], ALU.add))
        V(lambda v: v.tensor_scalar(mid[:], mid[:], 0.5, None, ALU.mult))
        nm = V(lambda v: v.tensor_scalar(nmid[:], mid[:], -1.0, None, ALU.mult))
        sp = SC(lambda s: s.activation(sgn[:], lgs[:], AF.Sign, bias=nmid[:], accum_out=cnt[:]),
                w=[nm, ld_lgs, ("chv", kb.now("chv"))])
        tg = T(lambda t: t.matmul(psC[:, 0:1], SELE[:], cnt[:], start=True, stop=True),
               w=[sp, helpers, ("chv", kb.now("chv"))])
        V(lambda v: v.tensor_scalar(cnt[:], psC[:, 0:1], -28672.5, None, ALU.is_ge), w=[tg])
        V(lambda v: v.tensor_tensor(nmid[:], mid[:], lo[:], ALU.subtract))
        V(lambda v: v.scalar_tensor_tensor(lo[:], nmid[:], cnt[:], lo[:], ALU.mult, ALU.add))
        V(lambda v: v.tensor_tensor(nmid[:], hi[:], mid[:], ALU.subtract))
        V(lambda v: v.scalar_tensor_tensor(hi[:], nmid[:], cnt[:], mid[:], ALU.mult, ALU.add))
    thf = ("chv", kb.now("chv"))

    mkd = V(lambda v: v.tensor_scalar(mkg[:], lgs[:], lo[:], None, ALU.is_gt), w=[thf])
    V(lambda v: v.tensor_reduce(cnt[:], mkg[:], AX, ALU.add))
    bcd = V(lambda v: v.tensor_reduce(bc[:], mkg[:].rearrange("p (b q) -> p b q", q=128), AX, ALU.add))
    c2d = V(lambda v: v.tensor_tensor_scan(cum2[:], rstp[:], mkg[:], 0.0, ALU.mult, ALU.add))
    dcu = kb.dma("gpsimd", CUM2D.rearrange("(p b) q -> p b q", b=32),
                 cum2[:].rearrange("p (b q) -> p b q", q=128), wait=[c2d])

    for c_ in range(8):
        kb.dma("gpsimd", BCE[:, c_ * 32:(c_ + 1) * 32], bc[c_ * 16:(c_ + 1) * 16, :], wait=[bcd])
    beced = ("dq_gpsimd", kb.now("dq_gpsimd"))
    V(lambda v: v.memset(zz16[:], 0.0))
    bcu = V(lambda v: v.tensor_tensor_scan(BCUM[:], BCE[:], zz16[0:16, 0:256], 1.0, ALU.mult, ALU.add),
            w=[beced])
    kb.ops.pop(); kb.count["chv"] -= 1
    bcu = V(lambda v: v.tensor_tensor_scan(BCUM[:], BCE[:], zz16[0:16, 0:256], 0.0, ALU.add, ALU.add),
            w=[beced])
    bex = V(lambda v: v.tensor_tensor(BEX[:], BCUM[:], BCE[:], ALU.subtract))
    kb.dma("gpsimd", BEXD.rearrange("(e b) one -> e b one", e=16), BEX[:].rearrange("e (b one) -> e b one", one=1),
           wait=[bex])
    tb8 = T(lambda t: t.matmul(psC[:], SEL8[:], BCUM[:], start=True, stop=True), w=[bcu, helpers])
    V(lambda v: v.tensor_copy(BCX8[:], psC[:]), w=[tb8])
    V(lambda v: v.memset(bof[:], 0.0))
    for b in range(256):
        V(lambda v, b=b: v.scalar_tensor_tensor(bof[:], slotv[:], BCX8[:, b:b + 1], bof[:],
                                                ALU.is_ge, ALU.add))
    bofd = ("chv", kb.now("chv"))
    e256 = kb.sbuf(u + "e256", [128, 1], F32)
    V(lambda v: v.tensor_copy(msk1[:], rd8[:]))
    V(lambda v: v.tensor_scalar(msk1[:], msk1[:], 8, None, ALU.logical_shift_left))
    V(lambda v: v.tensor_copy(e256[:], msk1[:]))
    V(lambda v: v.tensor_scalar(idxb[:], bof[:], e256[:], None, ALU.add))
    V(lambda v: v.tensor_copy(ii1[:], bof[:]))
    V(lambda v: v.tensor_scalar(ii2[:], ii1[:], 5, None, ALU.arith_shift_right))
    V(lambda v: v.tensor_scalar(ii2[:], ii2[:], 9, None, ALU.logical_shift_left))
    V(lambda v: v.tensor_scalar(ii1[:], ii1[:], 31, None, ALU.bitwise_and))
    V(lambda v: v.tensor_tensor(ii1[:], ii1[:], ii2[:], ALU.add))
    V(lambda v: v.tensor_copy(cridf[:], ii1[:]))
    e32 = kb.sbuf(u + "e32", [128, 1], F32)
    V(lambda v: v.tensor_copy(msk1[:], rd8[:]))
    V(lambda v: v.tensor_scalar(msk1[:], msk1[:], 5, None, ALU.logical_shift_left))
    V(lambda v: v.tensor_copy(e32[:], msk1[:]))
    crd = V(lambda v: v.tensor_scalar(cridf[:], cridf[:], e32[:], None, ALU.add))
    kb.dma("gpsimd", IDXBD.rearrange("e (c f) -> (e c) f", c=8), idxb[:],
           wait=[("chv", kb.now("chv"))])
    kb.dma("gpsimd", CRIDD.rearrange("e (c f) -> (e c) f", c=8), cridf[:], wait=[crd])
    tabd = ("dq_gpsimd", kb.now("dq_gpsimd"))
    lm2 = kb.dma("sync", my2[:], coreid2_d)
    V(lambda v: v.tensor_copy(my2f[:], my2[:]), w=[lm2])
    tmv = T(lambda t: t.transpose(psT[0:1, 0:2], my2f[:], cm.ident_f32[0:2, 0:2]),
            w=[("chv", kb.now("chv"))])
    V(lambda v: v.tensor_copy(my2r[:], psT[0:1, 0:2]), w=[tmv])
    t2v = T(lambda t: t.matmul(psT[:, 0:2], cm.ones_f32_row[:], my2r[:], start=True, stop=True),
            w=[("chv", kb.now("chv"))])
    V(lambda v: v.tensor_scalar(e256c[:], psT[:, 0:2], 256.0, None, ALU.mult), w=[t2v])
    ig1 = IG(myext[:, 0:CAP], IDXBD, my2[:], w=[tabd, ("chv", kb.now("chv"))])
    ig2 = IG(myext[:, CAP:2 * CAP], CRIDD, my2[:], w=[tabd])
    for tt_ in range(16):
        tr = T(lambda t, tt_=tt_: t.transpose(psT[:, 0:2], myext[:, tt_ * 128:(tt_ + 1) * 128],
                                              cm.ident_f32[0:2, 0:2]), w=[ig1, ig2, ("chv", kb.now("chv"))])
        V(lambda v, tt_=tt_: v.tensor_copy(myIB[:, 0 * 16 + tt_:0 * 16 + tt_ + 1], psT[:, 0:1]), w=[tr])
        V(lambda v, tt_=tt_: v.tensor_copy(myIB[:, 1 * 16 + tt_:1 * 16 + tt_ + 1], psT[:, 1:2]))
        tr2 = T(lambda t, tt_=tt_: t.transpose(psT[:, 0:2], myext[:, CAP + tt_ * 128:CAP + (tt_ + 1) * 128],
                                               cm.ident_f32[0:2, 0:2]), w=[("chv", kb.now("chv"))])
        V(lambda v, tt_=tt_: v.tensor_copy(myCR[:, 0 * 16 + tt_:0 * 16 + tt_ + 1], psT[:, 0:1]), w=[tr2])
        V(lambda v, tt_=tt_: v.tensor_copy(myCR[:, 1 * 16 + tt_:1 * 16 + tt_ + 1], psT[:, 1:2]))
    V(lambda v: v.tensor_copy(myIBi[:], myIB[:]))
    mycv = V(lambda v: v.tensor_copy(myCRi[:], myCR[:]))
    for el in range(2):
        for tt_ in range(16):
            col = el * 16 + tt_
            g1 = IG(bexg[:, col:col + 1], BEXD, myIBi[:, col:col + 1], w=[mycv, ("chv", kb.now("chv"))])
            g2 = IG(G2[:], CUM2D, myCRi[:, col:col + 1], w=[("chv", kb.now("chv")), ("dq_gpsimd", kb.now("dq_gpsimd"))])
            V(lambda v, tt_=tt_: v.tensor_scalar(sloc[:], riot[:], float(tt_ * 128), None, ALU.add),
              w=[g1, g2])
            V(lambda v, col=col: v.tensor_scalar(r2c[:], sloc[:], bexg[:, col:col + 1], None, ALU.subtract))
            V(lambda v: v.tensor_scalar(tmpc[:], G2[:], r2c[:], None, ALU.is_le))
            V(lambda v: v.tensor_reduce(posc[:], tmpc[:], AX, ALU.add))
            V(lambda v, col=col, el=el: v.tensor_scalar(r2c[:], myIB[:, col:col + 1],
                                                        e256c[:, el:el + 1], None, ALU.subtract))
            V(lambda v, col=col: v.scalar_tensor_tensor(xei[:, col:col + 1].bitcast(I32) if False else sloc[:],
                                                        r2c[:], 128.0, posc[:], ALU.mult, ALU.add))
            V(lambda v, col=col: v.tensor_copy(xei[:, col:col + 1], sloc[:]))
    xeid = ("chv", kb.now("chv"))

    xet = kb.sbuf(u + "xet", [128, D], BF16)
    xef = kb.sbuf(u + "xef", [128, DC * 512], BF16)
    w1s = kb.sbuf(u + "w1s", [128, DC * 512], BF16)
    w2s = kb.sbuf(u + "w2s", [128, 32 * 512], BF16)
    hfm = kb.sbuf(u + "hfm", [128, 32 * 512], BF16)
    f2br = kb.sbuf(u + "f2br", [1, D], BF16)
    ytile = kb.sbuf(u + "ytile", [128, 4 * D], BF16)
    psYs = [kb.psum(u + f"psY{i}", [128, 512], F32) for i in range(4)]
    zrow2 = kb.sbuf(u + "zrow2", [128, D], BF16)
    zm = V(lambda v: v.memset(zrow2[:], 0.0))
    kb.dma("gpsimd", YALL[8 * 2 * CAP:8 * 2 * CAP + 128, :], zrow2[:], wait=[zm])
    f1tile = kb.sbuf(u + "f1tile", [128, 2 * 32], F32)
    lb3 = kb.dma("sync", f1tile[:].rearrange("p (e f) -> p e f", e=2),
                 fc1b.rearrange("e p f -> p e f"))
    for el in range(2):
        V(lambda v, el=el: v.tensor_scalar(sloc[:], e256c[:, el:el + 1], 1.0 / 256.0, None, ALU.mult),
          w=[("chv", kb.now("chv"))])
        V(lambda v, el=el: v.tensor_scalar(ehot[:, el * E:(el + 1) * E], fio[:, 0:E], sloc[:], None, ALU.is_equal))
    ehd = ("chv", kb.now("chv"))
    for el in range(2):
        for mt in range(4):
            for i4 in range(4):
                col = el * 16 + mt * 4 + i4
                gx = IG(xet[:], XNALL, xei[:, col:col + 1], w=[cc2, xeid, ("pe", kb.now("pe")), ("chv", kb.now("chv"))])
                for kc in range(DC):
                    tr = T(lambda t, kc=kc: t.transpose(psTb[:], xet[:, kc * 128:(kc + 1) * 128], cm.ident_bf[:]),
                           w=[gx, ("chv", kb.now("chv"))])
                    V(lambda v, kc=kc, i4=i4: v.tensor_copy(xef[:, kc * 512 + i4 * 128:kc * 512 + (i4 + 1) * 128], psTb[:]),
                      w=[tr])
                gl = IG(lgrow[:], LGALL, xei[:, col:col + 1], w=[cc3])
                mx = V(lambda v: v.tensor_reduce(sloc[:], lgrow[:], AX, ALU.max), w=[gl])
                V(lambda v: v.tensor_scalar(r2c[:], sloc[:], -1.0, None, ALU.mult))
                sfx = SC(lambda s: s.activation(gprob[:], lgrow[:], AF.Exp, bias=r2c[:]),
                         w=[("chv", kb.now("chv"))])
                V(lambda v: v.tensor_reduce(sloc[:], gprob[:], AX, ALU.add), w=[sfx])
                V(lambda v: v.reciprocal(r2c[:], sloc[:]))
                V(lambda v, el=el: v.tensor_tensor(gprob[:], gprob[:], ehot[:, el * E:(el + 1) * E], ALU.mult),
                  w=[ehd])
                V(lambda v: v.tensor_reduce(sloc[:], gprob[:], AX, ALU.add))
                V(lambda v, el=el, mt=mt, i4=i4: v.tensor_tensor(gcol[:, (el * 16 + mt * 4 + i4):(el * 16 + mt * 4 + i4) + 1],
                                                                 sloc[:], r2c[:], ALU.mult))
            xefd = ("chv", kb.now("chv"))
            for fo in range(8):
                lw1 = kb.dma("gpsimd", w1s[:].rearrange("p (c f) -> p c f", f=512),
                             fc1T.rearrange("e (c p) f -> e p c f", p=128)[el, :, :, fo * 512:(fo + 1) * 512],
                             wait=[("pe", kb.now("pe"))], sname="dq_wl")
                for fs in range(4):
                    for c_ in range(DC):
                        kb.op("tensor", lambda t, fs=fs, c_=c_: t.matmul(
                            psF[:], w1s[:, c_ * 512 + fs * 128:c_ * 512 + (fs + 1) * 128],
                            xef[:, c_ * 512:(c_ + 1) * 512], start=(c_ == 0), stop=(c_ == DC - 1)),
                            wait=[xefd, lw1, ("chs", kb.now("chs"))] if c_ == 0 else (),
                            inc=[("pe", 1)] if c_ == DC - 1 else ())
                    pf = ("pe", kb.now("pe"))
                    fg = fo * 4 + fs
                    SC(lambda s, fg=fg, el=el: s.activation(hfm[:, fg * 512:(fg + 1) * 512], psF[:],
                                                            AF.Gelu, bias=f1tile[:, el * 32 + fg:el * 32 + fg + 1]),
                       w=[pf, lb3])
            hd = ("chs", kb.now("chs"))
            kb.dma("gpsimd", f2br[:], fc2b[el, :, :], wait=[("pe", kb.now("pe"))])
            f2d = ("dq_gpsimd", kb.now("dq_gpsimd"))
            for half in range(2):
                lw2 = kb.dma("gpsimd", w2s[:].rearrange("p (k w) -> p k w", w=512),
                             fc2T.rearrange("e (k p) d2 -> e p k d2", p=128)[el, :, :,
                                                                           half * 512:(half + 1) * 512],
                             wait=[("pe", kb.now("pe"))], sname="dq_wl")
                for st in range(4):
                    ps = psYs[st % 2]
                    for fc_ in range(32):
                        kb.op("tensor", lambda t, st=st, fc_=fc_, ps=ps: t.matmul(
                            ps[:], hfm[:, fc_ * 512 + st * 128:fc_ * 512 + (st + 1) * 128],
                            w2s[:, fc_ * 512:(fc_ + 1) * 512],
                            start=(fc_ == 0), stop=False),
                            wait=[hd, lw2, ("chv", kb.now("chv"))] if fc_ == 0 else (),
                            inc=())
                    kb.op("tensor", lambda t, ps=ps, half=half: t.matmul(
                        ps[:], cm.ones_bf_row[:], f2br[:, half * 512:(half + 1) * 512],
                        start=False, stop=True), wait=[f2d], inc=[("pe", 1)])
                    py = ("pe", kb.now("pe"))
                    V(lambda v, st=st, half=half, el=el, mt=mt, ps=ps: v.tensor_scalar(
                        ytile[:, st * D + half * 512:st * D + (half + 1) * 512], ps[:],
                        gcol[:, el * 16 + mt * 4 + st:el * 16 + mt * 4 + st + 1], None, ALU.mult),
                      w=[py, ("dq_gpsimd", kb.now("dq_gpsimd"))])
            yd = ("chv", kb.now("chv"))
            kb.dma("gpsimd", YLOC[el * CAP + mt * 512:el * CAP + (mt + 1) * 512, :].rearrange("(s p) d2 -> p s d2", p=128),
                   ytile[:].rearrange("p (s d2) -> p s d2", d2=D), wait=[yd])
    ydone = ("dq_gpsimd", kb.now("dq_gpsimd"))
    cc4 = CC("AllGather", ALU.bypass, [YLOC], [YALL[0:8 * 2 * CAP, :]], w=[ydone])

    TRIL = cm.tril
    thb = kb.sbuf(u + "thb", [128, E], F32)
    throw = kb.sbuf(u + "throw", [1, E], F32)
    roff = kb.sbuf(u + "roff", [1, E], F32)
    mkt = kb.sbuf(u + "mkt", [128, E], F32)
    mkv = kb.sbuf(u + "mkv", [128, E], F32)
    rnk = kb.sbuf(u + "rnk", [128, E], F32)
    HC = kb.sbuf(u + "HC", [128, E], F32)
    zz2 = kb.sbuf(u + "zz2", [128, E], F32)
    GS = kb.sbuf(u + "GS", [128, 32], F32)
    SRC = kb.sbuf(u + "SRC", [128, 32 * E], F32)
    GV = kb.sbuf(u + "GV", [128, 32 * E], F32)
    SRCi = kb.sbuf(u + "SRCi", [128, 32 * E], I32)
    jrow = kb.sbuf(u + "jrow", [128, E], F32)
    hm1 = kb.sbuf(u + "hm1", [128, 1], F32)
    vcol = kb.sbuf(u + "vcol", [128, 1], F32)
    ind = kb.sbuf(u + "ind", [128, E], F32)
    nall = kb.sbuf(u + "nall", [128, 1], F32)
    SCC = kb.sbuf(u + "SCC", [128, E], F32)
    tth = T(lambda t: t.transpose(psT[0:1, 0:16], lo[0:16, :], cm.ident_f32[0:16, 0:16]),
            w=[thf, ("chv", kb.now("chv"))])
    V(lambda v: v.tensor_copy(throw[:], psT[0:1, 0:16]), w=[tth])
    tbb = T(lambda t: t.matmul(psT[:, 0:E], cm.ones_f32_row[:], throw[:], start=True, stop=True),
            w=[("chv", kb.now("chv"))])
    V(lambda v: v.tensor_copy(thb[:], psT[:, 0:E]), w=[tbb])
    V(lambda v: v.memset(zz2[:], 0.0))
    kb.op("gpsimd", lambda g: g.iota(jrow[:], pattern=[[1, E]], base=0, channel_multiplier=0,
                                     allow_small_or_imprecise_dtypes=True),
          wait=[("chv", kb.now("chv"))], inc=[("chp", 1)])
    jd = ("chp", kb.now("chp"))
    nv = V(lambda v: v.tensor_reduce(nall[:], mkg[:], AX, ALU.add), w=[mkd, jd])
    V(lambda v: v.tensor_copy(msk1[:], riox[:]))
    V(lambda v: v.tensor_scalar(msk1[:], msk1[:], 4, None, ALU.arith_shift_right))
    V(lambda v: v.tensor_copy(sloc[:], msk1[:]))
    V(lambda v: v.tensor_scalar(r2c[:], e256c[:, 0:1], 1.0 / 512.0, None, ALU.mult))
    V(lambda v: v.tensor_tensor(sloc[:], sloc[:], r2c[:], ALU.is_lt))
    V(lambda v: v.tensor_scalar(SCC[:], jrow[:], rm16[:], None, ALU.is_equal))
    scd = V(lambda v: v.tensor_scalar(SCC[:], SCC[:], sloc[:], None, ALU.mult))
    tof = T(lambda t: t.matmul(psT[0:16, 0:1], SCC[:], nall[:], start=True, stop=True), w=[scd, nv])
    V(lambda v: v.tensor_copy(mkt[0:16, 0:1], psT[0:16, 0:1]), w=[tof])
    tof2 = T(lambda t: t.transpose(psT[0:1, 0:16], mkt[0:16, 0:1], cm.ident_f32[0:16, 0:16]),
             w=[("chv", kb.now("chv"))])
    V(lambda v: v.tensor_copy(roff[:], psT[0:1, 0:16]), w=[tof2])
    rod = ("chv", kb.now("chv"))

    NT128 = NTOK // 128
    for i in range(NT128):
        ll = kb.dma("sync", ltok[:], logits_tm[i * 128:(i + 1) * 128, :],
                    wait=[("chv", kb.now("chv")), ("pe", kb.now("pe"))])
        mv = V(lambda v: v.tensor_tensor(mkt[:], ltok[:], thb[:], ALU.is_gt), w=[ll, rod])
        t1_ = T(lambda t: t.matmul(psC[:, 0:E], TRIL[:], mkt[:], start=True, stop=False),
                w=[mv, ("chv", kb.now("chv"))])
        t2_ = T(lambda t: t.matmul(psC[:, 0:E], cm.ones_f32_row[:], roff[:], start=False, stop=True),
                w=[rod])
        pr_ = ("pe", kb.now("pe"))
        rv = V(lambda v: v.tensor_copy(rnk[:], psC[:, 0:E]), w=[pr_])
        t3_ = T(lambda t: t.matmul(psC[0:1, 0:E], cm.ones_f32_col[:], mkt[:], start=True, stop=True),
                w=[rv])
        V(lambda v: v.tensor_tensor(roff[:], roff[:], psC[0:1, 0:E], ALU.add), w=[t3_])
        V(lambda v: v.tensor_scalar(mkv[:], rnk[:], 2047.5, None, ALU.is_lt))
        V(lambda v: v.tensor_tensor(mkv[:], mkv[:], mkt[:], ALU.mult))
        mxv = V(lambda v: v.tensor_reduce(sloc[:], ltok[:], AX, ALU.max))
        V(lambda v: v.tensor_scalar(r2c[:], sloc[:], -1.0, None, ALU.mult))
        sfx = SC(lambda s: s.activation(gprob[:], ltok[:], AF.Exp, bias=r2c[:]),
                 w=[("chv", kb.now("chv"))])
        V(lambda v: v.tensor_reduce(sloc[:], gprob[:], AX, ALU.add), w=[sfx])
        V(lambda v: v.reciprocal(r2c[:], sloc[:]))
        gpd = V(lambda v: v.tensor_scalar(gprob[:], gprob[:], r2c[:], None, ALU.mult))
        V(lambda v: v.tensor_tensor(ind[:], gprob[:], mkv[:], ALU.mult))
        V(lambda v: v.tensor_reduce(sloc[:], ind[:], AX, ALU.add))
        V(lambda v: v.tensor_scalar(r2c[:], sloc[:], 0.0, None, ALU.is_gt))
        V(lambda v: v.tensor_scalar(posc[:], r2c[:], -1.0, 1.0, ALU.mult, ALU.add))
        V(lambda v: v.tensor_tensor(posc[:], posc[:], sloc[:], ALU.add))
        V(lambda v: v.reciprocal(posc[:], posc[:]))
        gsd = V(lambda v, i=i: v.tensor_tensor(GS[:, i:i + 1], posc[:], r2c[:], ALU.mult))
        hcd = V(lambda v: v.tensor_tensor_scan(HC[:], mkv[:], zz2[:], 0.0, ALU.add, ALU.add))
        V(lambda v, i=i: v.memset(SRC[:, i * E:(i + 1) * E], float(8 * 2 * CAP)))
        V(lambda v, i=i: v.memset(GV[:, i * E:(i + 1) * E], 0.0))
        for e in range(E):
            V(lambda v, e=e: v.tensor_scalar(hm1[:], HC[:, e:e + 1], -1.0, None, ALU.add))
            V(lambda v, e=e: v.scalar_tensor_tensor(hm1[:], mkv[:, e:e + 1], 1000.0, hm1[:], ALU.mult, ALU.add))
            V(lambda v: v.tensor_scalar(hm1[:], hm1[:], -1000.0, None, ALU.add))
            iv = V(lambda v: v.tensor_scalar(ind[:], jrow[:], hm1[:], None, ALU.is_equal))
            V(lambda v, e=e: v.tensor_scalar(vcol[:], rnk[:, e:e + 1], float(e * CAP - 8 * 2 * CAP), None, ALU.add))
            V(lambda v, i=i, e=e: v.scalar_tensor_tensor(
                SRC[:, i * E:(i + 1) * E], ind[:], vcol[:], SRC[:, i * E:(i + 1) * E], ALU.mult, ALU.add))
            V(lambda v, i=i, e=e: v.scalar_tensor_tensor(
                GV[:, i * E:(i + 1) * E], ind[:], gprob[:, e:e + 1], GV[:, i * E:(i + 1) * E], ALU.mult, ALU.add))
    V(lambda v: v.tensor_copy(SRCi[:], SRC[:]))
    tabsd = ("chv", kb.now("chv"))

    stg = kb.sbuf(u + "stg", [128, D], BF16)
    accb = kb.sbuf(u + "accb", [128, D], BF16)
    nrm32 = kb.sbuf(u + "nrm32", [128, D], F32)
    mof = kb.sbuf(u + "mof", [128, DC * 128], F32)
    V(lambda v: v.memset(stg[:], 0.0))
    for i in range(NT128):
        for j in range(E):
            gq = IG(stg[:], YALL, SRCi[:, i * E + j:i * E + j + 1],
                    w=[cc4, tabsd, ("chv", kb.now("chv"))])
            if j == 0:
                V(lambda v: v.tensor_copy(accb[:], stg[:]), w=[gq])
            else:
                V(lambda v: v.tensor_tensor(accb[:], accb[:], stg[:], ALU.add), w=[gq])
        nd = V(lambda v, i=i: v.tensor_scalar(nrm32[:], accb[:], GS[:, i:i + 1], None, ALU.mult))
        for kc in range(DC):
            tr = T(lambda t, kc=kc: t.transpose(psT[:], nrm32[:, kc * 128:(kc + 1) * 128], cm.ident_f32[:]),
                   w=[nd, ("chv", kb.now("chv"))])
            V(lambda v, kc=kc: v.tensor_copy(mof[:, kc * 128:(kc + 1) * 128], psT[:]),
              w=[tr, ("dq_gpsimd", kb.now("dq_gpsimd"))])
        kb.dma("gpsimd", moe_fm.rearrange("(c p) t -> p c t", p=128)[:, :, i * 128:(i + 1) * 128],
               mof[:].rearrange("p (c t) -> p c t", t=128), wait=[("chv", kb.now("chv"))])
    return dict(lo=lo, bof=bof, xei=xei, SRC=SRC, GV=GV, GS=GS, myIB=myIB, myCR=myCR,
                bexg=bexg, BCUM=BCUM, BCE=BCE, YALL=YALL, gcol=gcol, idxb=idxb,
                XNALL=XNALL, hfm=hfm, xef=xef, IG=IG, V=V, stg=stg, f1tile=f1tile)



def emit_final(kb: KB, cm: Common, x1_fm, s1_fm, moe_fm, x2_fm, tag="fn"):
    u = kb.uniq(tag)
    T_ = 512
    a = kb.sbuf(u + "a", [128, T_], F32)
    b = kb.sbuf(u + "b", [128, T_], F32)
    c = kb.sbuf(u + "c", [128, T_], F32)
    for i in range(NTOK // T_):
        for ch in range(DC):
            l1 = kb.dma("sync", a[:], moe_fm.rearrange("(c p) t -> p c t", p=128)[:, ch, i * T_:(i + 1) * T_],
                        wait=[("chs", kb.now("chs")), ("chv", kb.now("chv")), ("dq_gpsimd", kb.now("dq_gpsimd"))])
            l2 = kb.dma("sync", b[:], s1_fm.rearrange("(c p) t -> p c t", p=128)[:, ch, i * T_:(i + 1) * T_])
            l3 = kb.dma("sync", c[:], x1_fm.rearrange("(c p) t -> p c t", p=128)[:, ch, i * T_:(i + 1) * T_])
            s1_ = kb.op("scalar", lambda s: s.activation(a[:], a[:], AF.Silu),
                        wait=[l1, ("chs", kb.now("chs"))], inc=[("chs", 1)])
            sd = ("chs", kb.now("chs"))
            kb.op("vector", lambda v: v.tensor_tensor(a[:], a[:], b[:], ALU.mult),
                  wait=[sd, l2, ("chv", kb.now("chv"))], inc=[("chv", 1)])
            vd = kb.op("vector", lambda v: v.tensor_tensor(a[:], a[:], c[:], ALU.add),
                       wait=[l3, ("chv", kb.now("chv"))], inc=[("chv", 1)])
            kb.dma("gpsimd", x2_fm.rearrange("(c p) t -> p c t", p=128)[:, ch, i * T_:(i + 1) * T_],
                   a[:], wait=[("chv", kb.now("chv"))])
    return


def _btile(b):
    return np.ascontiguousarray(b.reshape(DC, 128).T.astype(np.float32))


def kernel(**inp):
    import ml_dtypes
    X = np.asarray(inp["L0_attn_x"])
    NC = 8

    def shard_x(c):
        return np.ascontiguousarray(X[2 * c:2 * c + 2].reshape(NTOK, D).T)

    kb = KB()
    cm = Common(kb)
    x = kb.input("x", [D, NTOK], F32)
    fq = kb.input("fq", [64, 1], F32)
    dcv = kb.input("dc", [64, 1], F32)
    ipT = kb.input("ipT", [64, D], F32)
    opj = kb.input("opj", [64, D], F32)
    w1 = kb.input("w1", [D, D], F32); w2 = kb.input("w2", [D, D], F32); w3 = kb.input("w3", [D, D], F32)
    b1 = kb.input("b1", [128, DC], F32); b2 = kb.input("b2", [128, DC], F32); b3 = kb.input("b3", [128, DC], F32)
    wg = kb.input("wg", [384, D], F32); wu = kb.input("wu", [384, D], F32)
    wd = kb.input("wd", [128, 2816], F32)
    nw = kb.input("nw", [128, DC], F32)
    rw = kb.input("rw", [D, 16], F32)
    rb = kb.input("rb", [1, 16], F32)
    wqkvT = kb.input("wqkvT", [D, 3072], F32)
    bq = kb.input("bq", [128, DC], F32); bk = kb.input("bk", [128, DC], F32)
    bv = kb.input("bv", [1, D], F32)
    woutT = kb.input("woutT", [D, D], F32); bo = kb.input("bo", [128, DC], F32)
    cid = kb.input("cid", [2, 1], I32)
    f1 = kb.input("f1", [2, D, 4096], F32)
    f1b = kb.input("f1b", [2, 128, 32], F32)
    f2 = kb.input("f2", [2, 4096, D], F32)
    f2b = kb.input("f2b", [2, 1, D], F32)
    x2 = kb.output("x2_fm", [D, NTOK], F32)
    xn_fm = kb.dram("i_xnfm", [D, NTOK], BF16)
    xn_tm = kb.dram("i_xntm", [NTOK, D], BF16)
    xn_lo = kb.dram("i_xnlo", [NTOK, D], BF16)
    ss_fm = kb.dram("i_ssfm", [D, NTOK], BF16)
    s_fm = kb.dram("i_sfm", [D, NTOK], F32)
    g_fm = kb.dram("i_gfm", [D, NTOK], F32)
    x1_fm = kb.dram("i_x1fm", [D, NTOK], F32)
    xn1_fm = kb.dram("i_xn1fm", [D, NTOK], BF16)
    xnm_tm = kb.dram("i_xnmtm", [NTOK, D], BF16)
    logits = kb.dram("i_lg", [NTOK, 16], F32)
    s1_fm = kb.dram("i_s1fm", [D, NTOK], F32)
    moe_fm = kb.dram("i_mofm", [D, NTOK], F32)

    kb.push_scope(); emit_ln(kb, cm, x, xn_fm, xn_tm, xn_tm_lo=xn_lo); kb.barrier(); kb.pop_scope()
    kb.push_scope(); emit_conv(kb, cm, xn_fm, xn_tm, fq, dcv, ipT, opj, ss_fm); kb.barrier(); kb.pop_scope()
    kb.push_scope(); emit_l0attn(kb, cm, xn_fm, ss_fm, w1, b1, w2, b2, w3, b3, s_fm); kb.barrier(); kb.pop_scope()
    kb.push_scope(); emit_l0ffn(kb, cm, xn_tm, xn_lo, wg, wu, wd, g_fm); kb.barrier(); kb.pop_scope()
    kb.push_scope(); emit_mid(kb, cm, x, s_fm, g_fm, x1_fm, xn1_fm, xnm_tm, logits, nw, rw, rb); kb.barrier(); kb.pop_scope()
    kb.push_scope(); emit_attn(kb, cm, xn1_fm, wqkvT, bq, bk, bv, woutT, bo, s1_fm); kb.barrier(); kb.pop_scope()
    kb.push_scope(); emit_moe(kb, cm, logits, xnm_tm, cid, f1, f1b, f2, f2b, moe_fm); kb.barrier(); kb.pop_scope()
    kb.push_scope(); emit_final(kb, cm, x1_fm, s1_fm, moe_fm, x2); kb.pop_scope()

    final = [(s, kb.now(s)) for s in list(kb.count) if s.startswith("dq_")]
    nc = kb.finalize(final_waits=final)

    WG = inp["L0_ffn_w_gate"].astype(np.float32); WU = inp["L0_ffn_w_up"].astype(np.float32)
    WD = inp["L0_ffn_w_down"].astype(np.float32)

    def wslice(W, c):
        r0, r1_ = RSPLIT[c], RSPLIT[c + 1]
        out = np.zeros((384, D), np.float32)
        out[0:r1_ - r0] = W[r0:r1_]
        return out

    BQKV = inp["L1_attn_b_qkv"].astype(np.float32)
    FC1 = inp["L1_ffn_fc1_weight"].astype(np.float32); FC1B = inp["L1_ffn_fc1_bias"].astype(np.float32)
    FC2 = inp["L1_ffn_fc2_weight"].astype(np.float32); FC2B = inp["L1_ffn_fc2_bias"].astype(np.float32)
    common = {
        "fq": np.ascontiguousarray(inp["L0_attn_frequencies"].astype(np.float32)[:, None]),
        "dc": np.ascontiguousarray(inp["L0_attn_decays"].astype(np.float32)[:, None]),
        "ipT": np.ascontiguousarray(inp["L0_attn_in_proj"].astype(np.float32).T),
        "opj": np.ascontiguousarray(inp["L0_attn_out_proj"].astype(np.float32)),
        "w1": np.ascontiguousarray(inp["L0_attn_ss_W"].astype(np.float32).T),
        "w2": np.ascontiguousarray(inp["L0_attn_gate_W"].astype(np.float32).T),
        "w3": np.ascontiguousarray(inp["L0_attn_fin_W"].astype(np.float32).T),
        "b1": _btile(inp["L0_attn_ss_b"]), "b2": _btile(inp["L0_attn_gate_b"]),
        "b3": _btile(inp["L0_attn_fin_b"]),
        "nw": _btile(inp["L1_ffn_norm_weight"]),
        "rw": np.ascontiguousarray(inp["L1_ffn_router_weight"].astype(np.float32).T),
        "rb": np.ascontiguousarray(inp["L1_ffn_router_bias"].astype(np.float32)[None, :]),
        "wqkvT": np.ascontiguousarray(inp["L1_attn_w_qkv"].astype(np.float32).T),
        "bq": _btile(BQKV[0:1024]), "bk": _btile(BQKV[1024:2048]),
        "bv": np.ascontiguousarray(BQKV[None, 2048:3072]),
        "woutT": np.ascontiguousarray(inp["L1_attn_w_out"].astype(np.float32).T),
        "bo": _btile(inp["L1_attn_b_out"]),
    }
    in_maps = []
    for c in range(NC):
        m = dict(common)
        m["x"] = shard_x(c)
        m["wg"] = wslice(WG, c); m["wu"] = wslice(WU, c)
        m["wd"] = np.ascontiguousarray(WD[128 * c:128 * (c + 1)])
        m["cid"] = np.array([[2 * c], [2 * c + 1]], np.int32)
        m["f1"] = np.ascontiguousarray(np.stack([FC1[2 * c].T, FC1[2 * c + 1].T]))
        m["f1b"] = np.ascontiguousarray(np.stack([FC1B[2 * c].reshape(32, 128).T,
                                                  FC1B[2 * c + 1].reshape(32, 128).T]))
        m["f2"] = np.ascontiguousarray(np.stack([FC2[2 * c].T, FC2[2 * c + 1].T]))
        m["f2b"] = np.ascontiguousarray(FC2B[2 * c:2 * c + 2][:, None, :])
        in_maps.append(m)

    from concourse.bass_utils import run_bass_kernel_spmd
    res = run_bass_kernel_spmd(nc, in_maps, core_ids=list(range(NC)))
    kb.close()
    out = np.empty((16, 2048, D), np.float32)
    for c in range(NC):
        out[2 * c:2 * c + 2] = np.asarray(res.results[c]["x2_fm"]).T.reshape(2, 2048, D)
    return out
